# version 06
# baseline (speedup 1.0000x reference)
import numpy as np
import ml_dtypes
import concourse.bass as bass
import concourse.mybir as mybir
from concourse.bass_utils import run_bass_kernel_spmd

F32 = mybir.dt.float32
BF16 = mybir.dt.bfloat16
AF = mybir.ActivationFunctionType
ALU = mybir.AluOpType
AX = mybir.AxisListType
NPBF = ml_dtypes.bfloat16

H = 2048
NH = 16
HD = 128
T = 1024
TC = 8
CH = 16
NBLK = 4
NE = 64
LN_EPS = 1e-5
RMS_EPS = 1e-6

ENG2SEM = {"sync": "y", "gpsimd": "p", "scalar": "s", "vector": "v", "tensor": "t"}



def _tup(x):
    if x is None:
        return ()
    if isinstance(x, tuple) and len(x) == 2 and isinstance(x[0], str) \
            and isinstance(x[1], int):
        return (x,)
    return tuple(t for t in x if t is not None)

class Rec:
    def __init__(self):
        self.stream = {e: [] for e in ENG2SEM}
        self.counts = {k: 0 for k in ("y", "p", "s", "v", "t", "cc")}

    def o(self, eng, fn, deps=(), sem=None, n=1):
        if sem is None:
            sem = ENG2SEM[eng]
        self.counts[sem] = self.counts.get(sem, 0) + n
        self.stream[eng].append((fn, tuple(d for d in deps if d is not None), sem, n))
        return (sem, self.counts[sem])


def build(taps=()):
    nc = bass.Bass("TRN2", target_bir_lowering=False, dynamic_dma_scratch_size=16384)

    def din(name, shape, dt=F32):
        return nc.dram_tensor(name, shape, dt, kind="ExternalInput").ap()

    xs_d = din("xs", [T, H])
    wkv8_d = din("wkv8", [8, 16, 128, 512])
    wq16_d = din("wq16", [16, 16, 128, 128])
    wg16_d = din("wg16", [16, 16, 128, 128])
    wo8_d = din("wo8", [2, 8, 128, 2048])
    wr16_d = din("wr16", [16, 128, 64])
    rb_d = din("rb", [64, 1])
    eb_d = din("eb", [64, H])
    nw_d = din("nw", [128, 16])
    slopes_d = din("slopes", [1, 16])
    masks_d = din("masks", [128, 2])
    identb_d = din("identb", [128, 128], BF16)
    identf_d = din("identf", [128, 128])
    iotam_d = din("iotam", [1, 256])
    jneg_d = din("jneg", [1, 64])
    iotanm_d = din("iotanm", [128, 2])
    sel8_d = din("sel8", [8, 1024])
    invd_d = din("invd", [128, 1], BF16)
    idxt_d = din("idxt", [128, 512])
    maskt_d = din("maskt", [128, 512])
    onesr_d = din("onesr", [1, 128])
    onesrb_d = din("onesrb", [1, 128], BF16)
    one1_d = din("one1", [1, 1])

    out_d = nc.dram_tensor("out", [T, H], F32, kind="ExternalOutput").ap()
    tap_d = {}
    for tp, shape, dt in (
        ("xt", [128, CH * T], BF16), ("ksb", [128, TC * H], BF16),
        ("vsb", [128, TC * H], BF16), ("mix", [128, NH * T], BF16),
        ("gate", [128, CH * T], BF16), ("lg", [64, T], F32),
        ("gt", [128, TC * H], BF16), ("ratio", [128, T], BF16),
        ("mwt", [64, T], BF16), ("rstd", [128, 8], F32),
        ("stb", [NH, NBLK, 128, 128], F32), ("ccout", [2 * NH * HD, 128], F32),
        ("qbf0", [128, T], BF16), ("qbf1", [128, T], BF16),
        ("kt0", [128, T], BF16), ("kt1", [128, T], BF16),
        ("ddh1", [128, 512], BF16), ("qdh1", [128, 256], BF16),
        ("sf1", [128, 128], BF16), ("qkp1", [128, 512], BF16),
        ("stbh1", [128, 4, 128], F32),
        ("z", [128, NH * T], BF16), ("yv00", [128, 1024], F32),
    ):
        if tp in taps:
            tap_d[tp] = nc.dram_tensor("tap_" + tp, shape, dt, kind="ExternalOutput").ap()

    xlo_d = nc.dram_tensor("xlo_i", [T, H], BF16, kind="Internal").ap()
    dd_d = nc.dram_tensor("dd_i", [NH, 128, 512], BF16, kind="Internal").ap()
    qd_d = nc.dram_tensor("qd_i", [NH, 128, 256], BF16, kind="Internal").ap()
    stb_d = nc.dram_tensor("stb_i", [NH, NBLK, 128, 128], F32, kind="Internal").ap()
    ccin_d = nc.dram_tensor("ccin_i", [NH * HD, 128], F32, kind="Internal").ap()
    ccout_d = nc.dram_tensor("ccout_i", [2 * NH * HD, 128], F32, kind="Internal").ap()

    sbg = []

    def sb(name, shape, dt, **kw):
        g = nc.sbuf_tensor(name, shape, dt, **kw)
        t = g.__enter__()
        sbg.append(g)
        return t

    XT = sb("XT", [128, CH * T], BF16, is_transpose_dest=True)
    KSB = sb("KSB", [128, TC * H], BF16, is_transpose_dest=True)
    VSB = sb("VSB", [128, TC * H], BF16)
    MIX = sb("MIX", [128, NH * T], BF16)
    GATE = sb("GATE", [128, CH * T], BF16)
    WSM = [sb("WSM0", [128, 2048], BF16), sb("WSM1", [128, 2048], BF16)]
    AR = sb("AR", [128, 16384], BF16, is_transpose_dest=True)
    IDB = sb("IDB", [128, 128], BF16)
    IDF = sb("IDF", [128, 128], F32)
    ONESR = sb("ONESR", [1, 128], F32)
    ONESRB = sb("ONESRB", [1, 128], BF16)
    ONE1 = sb("ONE1", [1, 1], F32)
    JNEG = sb("JNEG", [1, 64], F32)
    IOTANM = sb("IOTANM", [128, 2], F32)
    INVD = sb("INVD", [128, 1], BF16)
    SLOPES = sb("SLOPES", [1, 16], F32)
    SNEGR = sb("SNEGR", [1, 16], F32)
    SBC = sb("SBC", [128, 16], F32)
    SNEGBC = sb("SNEGBC", [128, 16], F32)
    BDBC = sb("BDBC", [128, 16], F32)
    BDPBC = sb("BDPBC", [128, 64], F32)
    KDCOL = sb("KDCOL", [128, 32], F32)
    NWC = sb("NWC", [128, 16], F32)
    MASKSB = sb("MASKSB", [128, 2], F32)
    RBC = sb("RBC", [64, 1], F32)
    MWT = sb("MWT", [64, T], BF16)
    RATIOB = sb("RATIOB", [128, T], BF16)
    RAT8 = sb("RAT8", [128, 8], F32)
    RSTDC = sb("RSTDC", [128, 8], F32)
    STATS = sb("STATS", [128, 24], F32)
    TOPC = sb("TOPC", [128, 24], F32)
    ROWS = sb("ROWSC", [1, 144], F32)

    psg = [nc.psum_tensor("pb%d" % i, [128, 512], F32) for i in range(8)]
    PB = [g.__enter__() for g in psg]

    semnames = ["y", "p", "s", "v", "t", "cc", "cn", "xs0", "xs1", "xt0", "xt1",
                "xlo0", "xlo1", "l12", "wr", "sel", "ixm",
                "wkv0", "wkv1", "wkv2", "wsm0", "wsm1",
                "ddst", "qdst", "stbst", "sdst",
                "kt0", "kt1", "ddl0", "ddl1", "qdl0", "qdl1",
                "stbl0", "stbl1", "kvr0", "kvr1", "ebl",
                "gtt", "xh0", "xh1", "wol0", "wol1", "ost0", "ost1"]
    sems = {k: nc.alloc_semaphore("sem_" + k) for k in semnames}
    R = Rec()

    def arf32(lo, n):
        return AR[:, lo:lo + 2 * n].bitcast(F32)

    def arbf(lo, n):
        return AR[:, lo:lo + n]

    XTL = XT.rearrange("p (a c t) -> p a c t", a=TC, c=CH)
    KSBH = KSB.rearrange("p (h a d) -> p h a d", h=NH, a=TC)
    VSBL = VSB.rearrange("p (a c f) -> p a c f", a=TC, c=CH)
    VSB3 = VSB.rearrange("p (a o) -> p a o", a=TC)
    MIX3 = MIX.rearrange("p (h t) -> p h t", h=NH)
    GATE3 = GATE.rearrange("p (c t) -> p c t", c=CH)

    def ydma(out, in_, deps=(), transpose=False, sem=None):
        if transpose:
            return R.o("sync", lambda e, o=out, i=in_: e.dma_start(out=o, in_=i, transpose=True), deps, n=16, sem=sem)
        return R.o("sync", lambda e, o=out, i=in_: e.dma_start(out=o, in_=i), deps, n=16, sem=sem)

    def pdma(out, in_, deps=(), sem=None):
        return R.o("gpsimd", lambda e, o=out, i=in_: e.dma_start(out=o, in_=i), deps, n=16, sem=sem)

    bankg = [None] * 8

    def mm(bank_or_out, lhsT, rhs, start, stop, deps=(), bank=None):
        out = bank_or_out
        if bank is not None and start:
            deps = tuple(deps) + (bankg[bank],)
        return R.o("tensor", lambda e, o=out, l=lhsT, r=rhs, st=start, sp=stop:
                   e.matmul(o, l, r, start=st, stop=sp), deps)

    def tr(out, in_, ident, deps=(), bank=None):
        if bank is not None:
            deps = tuple(deps) + (bankg[bank],)
        return R.o("tensor", lambda e, o=out, i=in_, d=ident: e.transpose(o, i, d), deps)

    def act(out, in_, func, deps=(), bias=0.0, scale=1.0, accum_out=None):
        if func is AF.Copy and not (isinstance(bias, float) and isinstance(scale, float)):
            func = AF.Identity
        return R.o("scalar", lambda e, o=out, i=in_, f=func, b=bias, sc=scale, a=accum_out:
                   e.activation(o, i, f, bias=b, scale=sc, accum_out=a), deps)

    def vop(name, *args, deps=()):
        return R.o("vector", lambda e, nm=name, aa=args: getattr(e, nm)(*aa), deps)

    for dst, srcd in ((IDB[:], identb_d[:]), (IDF[:], identf_d[:]),
                      (ONESR[:], onesr_d[:]), (ONE1[:], one1_d[:]),
                      (JNEG[:], jneg_d[:]),
                      (IOTANM[:], iotanm_d[:]), (INVD[:], invd_d[:]),
                      (SLOPES[:], slopes_d[:]), (NWC[:], nw_d[:]),
                      (ONESRB[:], onesrb_d[:]),
                      (MASKSB[:], masks_d[:]), (RBC[:], rb_d[:])):
        CT = ydma(dst, srcd, sem="cn")
    ld_idb = ld_idf = ld_onesr = ld_one1 = ld_iotam = ld_jneg = CT
    ld_iotanm = ld_invd = ld_slopes = ld_nw = ld_masks = ld_rb = CT

    XSL = [arf32(4096 * i, 2048) for i in range(4)]
    XNB = [MIX[:, 0:2048], MIX[:, 2048:4096]]
    SQD = GATE[:, 2048:4096]
    XLO1 = GATE[:, 0:2048]

    xt_tok = [None] * TC
    xlo_st = [None] * TC
    ops = {tc: {} for tc in range(TC)}

    def p0_g1(tc):
        i = tc % 4
        c0 = 8 * (tc % 3)
        o = ops[tc]
        pv = ops.get(tc - 4, {})
        o["xld"] = ydma(XSL[i], xs_d[tc * 128:(tc + 1) * 128, :],
                        deps=(pv.get("x1"), pv.get("xlo")), sem="xs%d" % (i % 2))
        o["sum"] = vop("reduce_sum", STATS[:, c0:c0 + 1], XSL[i], AX.X,
                       deps=(o["xld"],))
        sqg = ops.get(tc - 1, {}).get("sq2")
        o["sq1"] = act(SQD[:, 0:1024], XSL[i][:, 0:1024], AF.Square,
                       deps=(o["xld"], sqg),
                       accum_out=STATS[:, c0 + 1:c0 + 2])
        o["sq2"] = act(SQD[:, 1024:2048], XSL[i][:, 1024:2048], AF.Square,
                       deps=(o["xld"],),
                       accum_out=STATS[:, c0 + 2:c0 + 3])

    def p0_g2(tc):
        i = tc % 4
        j = tc % 2
        c0 = 8 * (tc % 3)
        o = ops[tc]
        o["mean"] = vop("tensor_scalar", STATS[:, c0 + 3:c0 + 4],
                        STATS[:, c0:c0 + 1], 1.0 / H, None, ALU.mult,
                        deps=(o["sum"],))
        o["ex2"] = vop("tensor_scalar", STATS[:, c0 + 4:c0 + 5],
                       STATS[:, c0 + 1:c0 + 2], STATS[:, c0 + 2:c0 + 3], 1.0 / H,
                       ALU.add, ALU.mult, deps=(o["sq1"], o["sq2"]))
        o["m2"] = vop("tensor_tensor", STATS[:, c0 + 5:c0 + 6],
                      STATS[:, c0 + 3:c0 + 4], STATS[:, c0 + 3:c0 + 4],
                      ALU.mult, deps=(o["mean"],))
        o["var"] = vop("tensor_scalar", STATS[:, c0 + 6:c0 + 7],
                       STATS[:, c0 + 4:c0 + 5], STATS[:, c0 + 5:c0 + 6], LN_EPS,
                       ALU.subtract, ALU.add, deps=(o["ex2"], o["m2"]))
        o["ln"] = act(STATS[:, c0 + 7:c0 + 8], STATS[:, c0 + 6:c0 + 7],
                      AF.Ln, deps=(o["var"],))
        o["lnm"] = vop("tensor_scalar", STATS[:, c0 + 6:c0 + 7],
                       STATS[:, c0 + 7:c0 + 8], -0.5, None, ALU.mult,
                       deps=(o["ln"],))
        o["rstd"] = act(STATS[:, c0 + 7:c0 + 8], STATS[:, c0 + 6:c0 + 7],
                        AF.Exp, deps=(o["lnm"],))
        o["nm"] = vop("scalar_tensor_tensor", STATS[:, c0 + 5:c0 + 6],
                      STATS[:, c0 + 3:c0 + 4], -1.0,
                      STATS[:, c0 + 7:c0 + 8], ALU.mult, ALU.mult,
                      deps=(o["rstd"], o["m2"]))
        o["xn"] = act(XNB[j], XSL[i], AF.Copy,
                      deps=(o["rstd"], o["nm"], o["sq1"], o["sq2"],
                            ops.get(tc - 2, {}).get("xt"),
                            ops.get(tc - 2, {}).get("x1")),
                      bias=STATS[:, c0 + 5:c0 + 6], scale=STATS[:, c0 + 7:c0 + 8])
        o["xt"] = ydma(XTL[:, tc], XNB[j].rearrange("t (c p) -> t c p", c=CH),
                       deps=(o["xn"],), transpose=True, sem="xt%d" % j)
        xt_tok[tc] = o["xt"]

    def p0_g3(tc):
        i = tc % 4
        j = tc % 2
        c0 = 8 * (tc % 3)
        o = ops[tc]
        o["x1"] = vop("scalar_tensor_tensor", XLO1, XSL[i],
                      STATS[:, c0 + 7:c0 + 8], XNB[j], ALU.mult, ALU.subtract,
                      deps=(o["xn"], ops.get(tc - 1, {}).get("x2")))
        XLO2i = arbf(4096 * i, 2048)
        o["x2"] = R.o("gpsimd",
                      lambda e, oo=XLO2i, ii=XLO1, ss=STATS[:, c0 + 5:c0 + 6]:
                      e.tensor_scalar(oo, ii, ss, None, ALU.add),
                      deps=(o["x1"],))
        o["xlo"] = R.o("scalar",
                       lambda e, oo=xlo_d[tc * 128:(tc + 1) * 128, :], ii=XLO2i:
                       e.dma_start(out=oo, in_=ii),
                       deps=(o["x2"],), sem="xlo%d" % (i % 2), n=16)
        xlo_st[tc] = o["xlo"]

    for k in range(TC + 4):
        if k - 4 >= 0:
            p0_g3(k - 4)
        if 0 <= k - 2 < TC:
            p0_g2(k - 2)
        if k < TC:
            p0_g1(k)
    xt_all = tuple(xt_tok)
    xsl_g = [(ops[6]["x1"], ops[6]["xlo"]), (ops[7]["x1"], ops[7]["xlo"])]
    xnb_g = [(ops[6]["xt"], ops[6]["x1"]), (ops[7]["xt"], ops[7]["x1"])]
    xlo1_g = ops[7]["x2"]
    sq_last = ops[7]["sq2"]
    xn_last = (ops[6]["xt"], ops[7]["xt"], ops[6]["x1"], ops[7]["x1"])
    if "xt" in tap_d:
        ydma(tap_d["xt"], XT[:], deps=xt_all)

    sneg_tok = vop("tensor_scalar", SNEGR[:], SLOPES[:], -1.0, None, ALU.mult,
                   deps=(ld_slopes,))
    bdr_u = vop("tensor_scalar", ROWS[:, 80:96], SLOPES[:], -256.0, None,
                ALU.mult, deps=(ld_slopes,))
    bdrow = act(ROWS[:, 0:16], ROWS[:, 80:96], AF.Exp, deps=(bdr_u,))
    bdp_u = vop("tensor_tensor",
                ROWS[:, 16:80].rearrange("a (e j) -> a e j", e=16),
                JNEG[:].rearrange("a (e j) -> a e j", e=16),
                SLOPES[0:1, :].rearrange("a (e o) -> a e o", o=1).broadcast_to([1, 16, 4]),
                ALU.mult, deps=(ld_jneg, ld_slopes))
    bdprow = act(ROWS[:, 80:144], ROWS[:, 16:80], AF.Exp, deps=(bdp_u, bdrow))
    b1 = mm(PB[4][:, 0:16], ONESR[:], SLOPES[:], True, False,
            deps=(ld_onesr, ld_slopes), bank=4)
    b2 = mm(PB[4][:, 16:32], ONESR[:], ROWS[:, 0:16], False, False, deps=(bdrow,))
    b3 = mm(PB[4][:, 32:96], ONESR[:], ROWS[:, 80:144], False, True, deps=(bdprow,))
    e_sbc = vop("tensor_copy", SBC[:], PB[4][:, 0:16], deps=(b3,))
    e_bd = vop("tensor_copy", BDBC[:], PB[4][:, 16:32], deps=(b3,))
    e_bdp = vop("tensor_copy", BDPBC[:], PB[4][:, 32:96], deps=(b3,))
    bankg[4] = e_bdp
    e_sneg = vop("tensor_scalar", SNEGBC[:], SBC[:], -1.0, None, ALU.mult,
                 deps=(e_sbc,))
    kd_prev = None
    for h in range(NH):
        cs = 16 + (h % 2) * 2
        u = vop("tensor_scalar", STATS[:, cs:cs + 2], IOTANM[:],
                SBC[:, h:h + 1], None, ALU.mult,
                deps=(ld_iotanm, e_sbc, kd_prev))
        kd_prev = act(KDCOL[:, 2 * h:2 * h + 2], STATS[:, cs:cs + 2],
                      AF.Exp, deps=(u,))
    kd_all = kd_prev

    DDA = MIX[:, 4096:5120].bitcast(F32)
    DDB = MIX[:, 5120:6144].bitcast(F32)
    DDC = MIX[:, 6144:6656]
    IDXA = MIX[:, 6656:7680].bitcast(F32)
    MKA = MIX[:, 7680:8704].bitcast(F32)
    IOTA_A = MIX[:, 8704:9216].bitcast(F32)[0:1, :]
    QR16 = MIX[:, 9216:13312]

    ld_idxt = ydma(IDXA, idxt_d[:], deps=(), sem="ixm")
    ld_maskt = ydma(MKA, maskt_d[:], deps=(), sem="ixm")
    ld_iotam2 = ydma(IOTA_A, iotam_d[:], deps=(), sem="ixm")

    dd_st = [None] * NH
    qr_tok = [None] * NH
    dda_g = None
    ddb_g = None
    ddc_g = None
    for h in range(NH):
        u1 = vop("tensor_scalar", DDA, IDXA, SNEGBC[:, h:h + 1], None, ALU.mult,
                 deps=(ld_idxt, e_sneg, dda_g))
        u2 = vop("tensor_scalar", DDB, DDA, 0.0, None, ALU.min,
                 deps=(u1, ddb_g))
        u3 = act(DDA, DDB, AF.Exp, deps=(u2,))
        dd = vop("tensor_tensor", DDC, DDA, MKA, ALU.mult,
                 deps=(u3, ld_maskt, ddc_g))
        dd_st[h] = ydma(dd_d[h], DDC, deps=(dd,), sem="ddst")
        dda_g = dd
        ddb_g = u3
        ddc_g = dd_st[h]
        qr_tok[h] = act(QR16[0:1, h * 256:(h + 1) * 256], IOTA_A, AF.Exp,
                        deps=(ld_iotam2, sneg_tok), scale=SNEGR[0:1, h:h + 1])
    dd_all = dd_st[NH - 1]
    qr_all = qr_tok[NH - 1]

    WKV = [arbf(0, 8192).rearrange("p (c j) -> p c j", c=CH),
           arbf(8192, 8192).rearrange("p (c j) -> p c j", c=CH)]
    slot_g = [(ops[4]["x1"], ops[4]["xlo"], ops[5]["x1"], ops[5]["xlo"]),
              (xsl_g[0], xsl_g[1], xlo1_g)]
    wkv_ld = [None] * 8

    def wkv_load(ob):
        g = ()
        for x_ in _tup(slot_g[ob % 2]):
            g += _tup(x_)
        return pdma(WKV[ob % 2], wkv8_d[ob].rearrange("c p j -> p c j"),
                    deps=g, sem="wkv%d" % (ob % 2))

    for ob in range(2):
        wkv_ld[ob] = wkv_load(ob)
    ev_tok = {}
    for ob in range(8):
        obk = ob if ob < 4 else ob - 4
        for tg in range(2):
            lastmm = {}
            for c in range(CH):
                for k in range(4):
                    tc = tg * 4 + k
                    bank = tg * 4 + k
                    first = (c == 0)
                    deps = [wkv_ld[ob]]
                    if first:
                        deps += [xt_tok[tg * 4 + kk] for kk in range(4)]
                    lastmm[tc] = mm(PB[bank][:, :],
                                    XTL[:, tc, c, :],
                                    WKV[ob % 2][:, c, :],
                                    first, (c == CH - 1),
                                    deps=deps, bank=bank)
            for k in range(4):
                tc = tg * 4 + k
                if ob < 4:
                    dst = KSBH[:, 4 * obk:4 * obk + 4, tc, :]
                else:
                    dst = VSB3[:, tc, 512 * obk:512 * obk + 512]
                ev_tok[(ob, tc)] = act(dst, PB[tg * 4 + k][:, :], AF.Silu,
                                       deps=(lastmm[tc],))
                bankg[tg * 4 + k] = ev_tok[(ob, tc)]
        slot_g[ob % 2] = lastmm[7]
        if ob + 2 < 8:
            wkv_ld[ob + 2] = wkv_load(ob + 2)
    kv_all = tuple(ev_tok[(ob, tc)] for ob in range(8) for tc in range(TC))
    k_done = tuple(ev_tok[(ob, tc)] for ob in range(4) for tc in range(TC))
    p1_done = kv_all[-1]
    if "ksb" in tap_d:
        ydma(tap_d["ksb"], KSB[:], deps=k_done)
    if "vsb" in tap_d:
        ydma(tap_d["vsb"], VSB[:], deps=kv_all)

    arena_g0 = _tup(slot_g[0])
    arena_g1 = _tup(slot_g[1])

    XLOT = GATE[:, 2048:10240].rearrange("p (a c t) -> p a c t", a=4, c=CH)
    WRF = GATE[:, 10240:12288].bitcast(F32)
    WRHI = GATE[:, 12288:13312]
    WRLO = GATE[:, 13312:14336]
    SEL8A = GATE[:, 14336:16384].bitcast(F32)[0:8, :]
    WRF3 = WRF.rearrange("p (c j) -> p c j", c=CH)
    WRHI3 = WRHI.rearrange("p (c j) -> p c j", c=CH)
    WRLO3 = WRLO.rearrange("p (c j) -> p c j", c=CH)

    ld_wrf = ydma(WRF3, wr16_d.rearrange("c p j -> p c j"), deps=(), sem="wr")
    wr_hi = act(WRHI, WRF, AF.Copy, deps=(ld_wrf,))
    wr_lo = vop("tensor_tensor", WRLO, WRF, WRHI, ALU.subtract, deps=(wr_hi,))
    ld_sel8 = ydma(SEL8A, sel8_d[:], deps=(), sem="sel")

    lg_mm_last = [None, None]
    prev_l2 = None
    for hf in range(2):
        tok_tr = []
        for tci in range(4):
            tc = hf * 4 + tci
            l2 = ydma(
                XLOT[:, tci],
                xlo_d[tc * 128:(tc + 1) * 128, :].rearrange("t (c p) -> t c p", c=CH),
                deps=(xlo_st[tc], prev_l2) +
                ((sq_last,) if hf == 0 and tci == 0 else ()) +
                ((lg_mm_last[0],) if hf == 1 else ()),
                transpose=True, sem="l12")
            prev_l2 = l2
            tok_tr.append(l2)
        first = True
        lastmm = None
        for c in range(CH):
            xhi = XTL[:, 4 * hf:4 * hf + 4, c, :]
            xlo = XLOT[:, :, c, :]
            whi = WRHI3[:, c, :]
            wlo = WRLO3[:, c, :]
            d0 = (wr_hi,) + xt_all + tuple(tok_tr) if first else (wr_hi,)
            lastmm = mm(PB[5 + hf][0:64, :], whi, xhi, first, False, deps=d0,
                        bank=5 + hf)
            first = False
            lastmm = mm(PB[5 + hf][0:64, :], wlo, xhi, False, False, deps=(wr_lo,))
            lastmm = mm(PB[5 + hf][0:64, :], whi, xlo, False, (c == CH - 1),
                        deps=(tok_tr[-1],))
        lg_mm_last[hf] = lastmm

    LGSB = arf32(0, 1024)[0:64, :]
    lg_tok = []
    for hf in range(2):
        lg_tok.append(act(LGSB[:, hf * 512:(hf + 1) * 512], PB[5 + hf][0:64, :],
                          AF.Copy, deps=(lg_mm_last[hf], ld_rb, wr_lo), bias=RBC[:]))
    bankg[5] = lg_tok[0]
    bankg[6] = lg_tok[1]
    if "lg" in tap_d:
        ydma(tap_d["lg"], LGSB, deps=tuple(lg_tok))

    E_t = arf32(2048, 64)
    MB_t = arbf(2176, 64)
    MS_t = arf32(2240, 64)
    MS2_t = arf32(2368, 64)
    mwt_tok = [None] * TC
    rat_tok = [None] * TC
    prev_lcp = None
    prev_ee = None
    prev_mwb = None
    for tc in range(TC):
        hf = tc // 4
        ltr = tr(PB[5][:, 0:64], LGSB[:, tc * 128:(tc + 1) * 128], IDF[0:64, 0:64],
                 deps=(lg_tok[hf], ld_idf, prev_lcp), bank=5)
        lcp = vop("tensor_copy", MS_t, PB[5][:, 0:64],
                  deps=(ltr, prev_mwb) + (arena_g0 if tc == 0 else ()))
        bankg[5] = lcp
        nmx = vop("reduce_max", TOPC[:, 0:1], MS_t, AX.X, None, None, True,
                  deps=(lcp,))
        ee = act(E_t, MS_t, AF.Exp,
                 deps=(nmx, prev_ee) + (arena_g0 if tc == 0 else ()),
                 bias=TOPC[:, 0:1])
        t8 = vop("max", TOPC[:, 2:10], E_t, deps=(ee,))
        den = vop("reduce_sum", TOPC[:, 10:11], E_t, AX.X, deps=(ee,))
        rz = vop("reciprocal", TOPC[:, 11:12], TOPC[:, 10:11], deps=(den,))
        s8 = vop("reduce_sum", TOPC[:, 12:13], TOPC[:, 2:10], AX.X, deps=(t8,))
        rat_tok[tc] = vop("tensor_tensor", RAT8[:, tc:tc + 1], TOPC[:, 12:13],
                          TOPC[:, 11:12], ALU.mult, deps=(s8, rz))
        msk = vop("tensor_scalar", MS_t, E_t, TOPC[:, 9:10], None, ALU.is_ge,
                  deps=(t8, lcp))
        mw1 = vop("tensor_tensor", MS2_t, MS_t, E_t, ALU.mult,
                  deps=(msk, ee) + (arena_g0 if tc == 0 else ()))
        mwb = vop("tensor_scalar", MB_t, MS2_t, TOPC[:, 11:12], None, ALU.mult,
                  deps=(mw1, rz, prev_mwb) + (arena_g0 if tc == 0 else ()))
        mtr = tr(PB[6][0:64, 0:256].bitcast(BF16)[:, 0:128], MB_t, IDB[:],
                 deps=(mwb, ld_idb, mwt_tok[tc - 1] if tc else None), bank=6)
        mwt_tok[tc] = vop("tensor_copy", MWT[:, tc * 128:(tc + 1) * 128],
                          PB[6][0:64, 0:256].bitcast(BF16)[:, 0:128], deps=(mtr,))
        bankg[6] = mwt_tok[tc]
        prev_lcp = lcp
        prev_ee = mw1
        prev_mwb = mwb
    rtr = tr(PB[5][0:8, 0:128], RAT8[:], IDF[:],
             deps=tuple(rat_tok) + (ld_idf, prev_lcp), bank=5)
    RRS = MIX[:, 0:256].bitcast(F32)[0:8, :]
    rcp = vop("tensor_copy", RRS[:], PB[5][0:8, 0:128], deps=(rtr,) + xn_last)
    bankg[5] = rcp
    rb_mm = None
    for tcq in range(2):
        for k in range(4):
            tc = tcq * 4 + k
            rb_mm = mm(PB[6][:, (tcq * 4 + k - tcq * 4) * 128:
                             (tcq * 4 + k - tcq * 4 + 1) * 128]
                       if False else
                       PB[6][:, k * 128:(k + 1) * 128],
                       SEL8A[:, tc * 128:(tc + 1) * 128], RRS[:],
                       k == 0, k == 3, deps=(rcp, ld_sel8), bank=6)
        ratio_tok_i = act(RATIOB[:, tcq * 512:(tcq + 1) * 512], PB[6][:, :],
                          AF.Copy, deps=(rb_mm,))
        bankg[6] = ratio_tok_i
        if tcq == 0:
            ratio_tok = [ratio_tok_i]
        else:
            ratio_tok.append(ratio_tok_i)
    if "ratio" in tap_d:
        ydma(tap_d["ratio"], RATIOB[:], deps=tuple(ratio_tok))
    if "mwt" in tap_d:
        ydma(tap_d["mwt"], MWT[:], deps=tuple(mwt_tok))

    QDBALL = arbf(11520, 4096)
    qdb_tok = [None] * NH
    qe_g = _tup(slot_g[1])
    for h in range(NH):
        qb_mm = mm(PB[4][:, 0:256], ONESRB[:], QR16[0:1, h * 256:(h + 1) * 256],
                   True, True, deps=(qr_tok[h],), bank=4)
        qdb_tok[h] = act(QDBALL[:, h * 256:(h + 1) * 256], PB[4][:, 0:256],
                         AF.Copy, deps=(qb_mm,) + qe_g)
        qe_g = ()
        bankg[4] = qdb_tok[h]

    KD = [arbf(10240, 128), arbf(10368, 128)]
    STT = [arf32(10496, 128), arf32(10752, 128)]
    SEND = [arf32(11008, 128), arf32(11264, 128)]
    chain_arena_g = arena_g1

    send_tok = [None] * NH
    stb_store = {}
    st_chain = {"stt_g": [chain_arena_g, chain_arena_g],
                "send_g": [chain_arena_g, chain_arena_g],
                "kd_g": [chain_arena_g, chain_arena_g],
                "pbU": None}

    def emit_chain_step(h, j, cst):
        if j == 0:
            cst["ms0"] = vop("memset", STT[0], 0.0, deps=_tup(cst["stt_g"][0]))
            cst["cur"], cst["oth"] = 0, 1
            cst["cur_tok"] = cst["ms0"]
        cur, oth = cst["cur"], cst["oth"]
        stb_store[(h, j)] = ydma(stb_d[h, j], STT[cur], deps=(cst["cur_tok"],),
                                 sem="stbst")
        kd0 = act(KD[0], KSBH[:, h, 2 * j, :], AF.Copy,
                  deps=(ev_tok[(h // 4, 2 * j)], kd_all) + _tup(cst["kd_g"][0]),
                  scale=KDCOL[:, 2 * h:2 * h + 1])
        kd1 = act(KD[1], KSBH[:, h, 2 * j + 1, :], AF.Copy,
                  deps=(ev_tok[(h // 4, 2 * j + 1)],) + _tup(cst["kd_g"][1]),
                  scale=KDCOL[:, 2 * h + 1:2 * h + 2])
        m1 = mm(PB[7][:, 0:128], KD[0], VSB3[:, 2 * j, 128 * h:128 * h + 128],
                True, False,
                deps=(kd0, ev_tok[(4 + h // 4, 2 * j)], cst["pbU"]), bank=7)
        m2 = mm(PB[7][:, 0:128], KD[1], VSB3[:, 2 * j + 1, 128 * h:128 * h + 128],
                False, True, deps=(kd1, ev_tok[(4 + h // 4, 2 * j + 1)]))
        cst["kd_g"] = [m1, m2]
        up = vop("scalar_tensor_tensor", STT[oth], STT[cur], BDBC[:, h:h + 1],
                 PB[7][:, 0:128], ALU.mult, ALU.add,
                 deps=(m2, cst["cur_tok"], e_bd, stb_store[(h, j)],
                       stb_store.get((h, j - 1)) if j > 0 else
                       _tup(cst["stt_g"][1])[0] if _tup(cst["stt_g"][1]) else None))
        bankg[7] = up
        cst["pbU"] = up
        cst["cur"], cst["oth"] = oth, cur
        cst["cur_tok"] = up
        if j == NBLK - 1:
            sd = vop("tensor_scalar", SEND[h % 2], STT[cst["cur"]],
                     MASKSB[:, 0:1], None, ALU.mult,
                     deps=(up, ld_masks) + _tup(cst["send_g"][h % 2]))
            send_tok[h] = ydma(ccin_d[128 * h:128 * h + 128, :], SEND[h % 2],
                               deps=(sd,), sem="sdst")
            cst["send_g"][h % 2] = send_tok[h]
            cst["stt_g"] = [sd, stb_store[(h, 3)]]

    gate_tok = {}
    wsm_g = [None, None]
    wsm_n = [0]

    def wsm_load(src_ap, deps=()):
        i = wsm_n[0] % 2
        wsm_n[0] += 1
        t = pdma(WSM[i].rearrange("p (c j) -> p c j", c=CH),
                 src_ap.rearrange("c p j -> p c j"),
                 deps=_tup(wsm_g[i]) + tuple(deps), sem="wsm%d" % i)
        return i, t

    for i_ in range(CH):
        wi, wt = wsm_load(wg16_d[i_])
        pbase = 0 if i_ % 2 == 0 else 2
        lastmm = [None, None]
        for c in range(CH):
            for th in range(2):
                first = (c == 0)
                lastmm[th] = mm(PB[pbase + th][:, :],
                                WSM[wi][:, c * 128:(c + 1) * 128],
                                XTL[:, 4 * th:4 * th + 4, c, :],
                                first, c == CH - 1,
                                deps=(wt,), bank=pbase + th)
            if c % 4 == 3:
                emit_chain_step(i_, c // 4, st_chain)
        if i_ < 2:
            gdep = (xlo1_g, prev_mwb)
        elif i_ < 10:
            gdep = (lg_mm_last[1],)
        elif i_ < 14:
            gdep = (lg_mm_last[1], wr_lo)
        else:
            gdep = (rb_mm,)
        for th in range(2):
            gate_tok[(i_, th)] = act(GATE3[:, i_, th * 512:(th + 1) * 512],
                                     PB[pbase + th][:, :], AF.Sigmoid,
                                     deps=(lastmm[th],) + gdep)
            bankg[pbase + th] = gate_tok[(i_, th)]
        wsm_g[wi] = lastmm[1]
    kd_g = st_chain["kd_g"]
    gate_all = tuple(gate_tok[(oc, th)] for oc in range(CH) for th in range(2))
    if "gate" in tap_d:
        ydma(tap_d["gate"], GATE[:], deps=gate_all)

    cc_tok = R.o("gpsimd",
                 lambda e: e.collective_compute(
                     "AllGather", ALU.bypass,
                     replica_groups=[[0, 1], [2, 3], [4, 5], [6, 7]],
                     ins=[ccin_d.opt()], outs=[ccout_d.opt()]),
                 deps=tuple(send_tok), sem="cc", n=1)
    if "ccout" in tap_d:
        ydma(tap_d["ccout"], ccout_d[:], deps=(cc_tok,))
    if "stb" in tap_d:
        ydma(tap_d["stb"], stb_d[:],
             deps=tuple(stb_store[(h, j)] for h in range(NH) for j in range(NBLK)))


    QBF = [arbf(0, 1024), arbf(1024, 1024)]
    QDQ = [arbf(2048, 1024), arbf(3072, 1024)]
    KT = [arbf(4096, 1024), arbf(5120, 1024)]
    DDH = [arbf(6144, 512), arbf(6656, 512)]
    STBH = [arf32(7680, 512), arf32(8704, 512)]
    KVH = [arf32(9728, 128), arf32(9984, 128)]
    KVR = [arf32(10240, 128), arf32(10496, 128)]
    SF = [arbf(10752, 128), arbf(10880, 128)]
    QKP = [arbf(11008, 256), arbf(11264, 256)]

    p4g0 = (dd_st[NH - 1], send_tok[NH - 1],
            stb_store[(NH - 1, 3)], kd_g[0], kd_g[1],
            prev_mwb, prev_lcp, rcp, lg_mm_last[1], ratio_tok[1])

    qbf_g = [p4g0, p4g0]
    qdq_g = [p4g0, p4g0]
    kt_g = [p4g0, p4g0]
    ddh_g = [p4g0, p4g0]
    stbh_g = [p4g0, p4g0]
    kvh_g = [p4g0, p4g0]
    kvr_g = [p4g0, p4g0]
    sf_g = [p4g0, p4g0]
    qkp_g = [p4g0, p4g0]

    head_pre = {}

    def prefetch_head(h, extra=()):
        i = h % 2
        toks = {}
        toks["w"] = wsm_load(wq16_d[h], deps=extra)
        toks["kt"] = ydma(KT[i].rearrange("d (a t) -> d a t", a=TC),
                          KSB[:, h * 1024:(h + 1) * 1024]
                          .rearrange("t (a d) -> t a d", a=TC),
                          deps=k_done + _tup(kt_g[i]) + tuple(extra),
                          transpose=True, sem="kt%d" % i)
        toks["dd"] = ydma(DDH[i], dd_d[h], deps=(dd_all,) + _tup(ddh_g[i]),
                          sem="ddl%d" % i)

        toks["stb"] = ydma(STBH[i].rearrange("p (j e) -> p j e", j=4),
                           stb_d[h].rearrange("j p e -> p j e"),
                           deps=(stb_store[(NH - 1, 3)],) + _tup(stbh_g[i]),
                           sem="stbl%d" % i)
        toks["kvr"] = ydma(KVR[i], ccout_d[128 * h:128 * h + 128, :],
                           deps=(cc_tok,) + _tup(kvr_g[i]), sem="kvr%d" % i)
        head_pre[h] = toks
        return toks

    prefetch_head(0)
    prefetch_head(1)

    mix_tok = {}
    for h in range(NH):
        i = h % 2
        pre = head_pre[h]
        wi = pre["w"][0]
        kvm = vop("tensor_scalar", KVH[i], KVR[i], MASKSB[:, 1:2], None, ALU.mult,
                  deps=(pre["kvr"], ld_masks) + _tup(kvh_g[i]))
        kvr_g[i] = kvm
        pbase = 0 if h % 2 == 0 else 2
        qlast = [None, None]
        for c in range(CH):
            for th in range(2):
                first = (c == 0)
                deps = [pre["w"][1]]
                if first:
                    deps += list(_tup(qbf_g[i]))
                qlast[th] = mm(PB[pbase + th][:, :],
                               WSM[wi][:, c * 128:(c + 1) * 128],
                               XTL[:, 4 * th:4 * th + 4, c, :],
                               first, c == CH - 1, deps=deps, bank=pbase + th)
        qsil = []
        for th in range(2):
            qsil.append(act(QBF[i][:, th * 512:(th + 1) * 512], PB[pbase + th][:, :],
                            AF.Silu, deps=(qlast[th],) + _tup(qbf_g[i])))
            bankg[pbase + th] = qsil[th]
        wsm_g[wi] = qlast[1]
        qdq = vop("tensor_tensor",
                  QDQ[i].rearrange("p (b m) -> p b m", b=NBLK),
                  QBF[i].rearrange("p (b m) -> p b m", b=NBLK),
                  QDBALL[:, h * 256:(h + 1) * 256]
                  .rearrange("p (b m) -> p b m", b=1).broadcast_to([128, NBLK, 256]),
                  ALU.mult,
                  deps=(qsil[0], qsil[1], qdb_tok[h]) + _tup(qdq_g[i]))
        if h in (0, 1):
            sfx = str(h)
            if "qbf" + sfx in tap_d:
                ydma(tap_d["qbf" + sfx], QBF[i], deps=(qsil[0], qsil[1]))
            if "kt" + sfx in tap_d:
                ydma(tap_d["kt" + sfx], KT[i], deps=(pre["kt"],))
            if h == 1 and "ddh1" in tap_d:
                ydma(tap_d["ddh1"], DDH[i], deps=(pre["dd"],))
            if h == 1 and "stbh1" in tap_d:
                ydma(tap_d["stbh1"].rearrange("p j e -> p j e"),
                     STBH[i].rearrange("p (j e) -> p j e", j=4), deps=(pre["stb"],))
        o_evs = []
        for j in range(NBLK):
            jp = j % 2
            sf = vop("scalar_tensor_tensor", SF[jp], KVH[i],
                     BDPBC[:, 4 * h + j:4 * h + j + 1],
                     STBH[i][:, 128 * j:128 * j + 128], ALU.mult, ALU.add,
                     deps=(kvm, pre["stb"], e_bdp) + _tup(sf_g[jp]))
            qks = []
            for ncn in range(2):
                qks.append(mm(PB[4 + ncn][:, 0:256],
                              KT[i][:, 256 * j + 128 * ncn:256 * j + 128 * ncn + 128],
                              QBF[i][:, 256 * j:256 * j + 256], True, True,
                              deps=(pre["kt"], qsil[0] if j < 2 else qsil[1]),
                              bank=4 + ncn))
            qkps = []
            for ncn in range(2):
                qkps.append(vop("tensor_tensor", QKP[ncn], PB[4 + ncn][:, 0:256],
                                DDH[i][:, 256 * ncn:256 * ncn + 256], ALU.mult,
                                deps=(qks[ncn], pre["dd"]) + _tup(qkp_g[ncn])))
                bankg[4 + ncn] = qkps[ncn]
            ob_ = 6 + jp
            mm(PB[ob_][:, 0:256], VSB3[:, 2 * j, 128 * h:128 * h + 128],
               QKP[0], True, False,
               deps=(qkps[0], ev_tok[(4 + h // 4, 2 * j)]), bank=ob_)
            mm(PB[ob_][:, 0:256], VSB3[:, 2 * j + 1, 128 * h:128 * h + 128],
               QKP[1], False, False, deps=(qkps[1], ev_tok[(4 + h // 4, 2 * j + 1)]))
            mo3 = mm(PB[ob_][:, 0:256], SF[jp], QDQ[i][:, 256 * j:256 * j + 256],
                     False, True, deps=(sf, qdq))
            sf_g[jp] = mo3
            qkp_g = [mo3, mo3]
            if h == 1 and j == 0 and "sf1" in tap_d:
                ydma(tap_d["sf1"], SF[jp], deps=(sf,))
            if h == 1 and j == 0 and "qkp1" in tap_d:
                ydma(tap_d["qkp1"][:, 0:256], QKP[0], deps=(qkps[0],))
                ydma(tap_d["qkp1"][:, 256:512], QKP[1], deps=(qkps[1],))
            mixg = ()
            if h < 4:
                mixg = xn_last
            elif h < 14:
                mixg = (dd_all, qdb_tok[NH - 1])
            ev = act(MIX3[:, h, 256 * j:256 * j + 256], PB[ob_][:, 0:256], AF.Copy,
                     deps=(mo3,) + mixg)
            bankg[ob_] = ev
            o_evs.append(ev)
        kt_g[i] = qks[1]
        ddh_g[i] = qkps[1]
        stbh_g[i] = sf
        kvh_g[i] = sf
        qbf_g[i] = (qks[1], qdq)
        qdq_g[i] = mo3
        if h + 2 < NH:
            prefetch_head(h + 2)
        mix_tok[h] = tuple(o_evs)
    mix_all = tuple(t for h in range(NH) for t in mix_tok[h])
    p4_done = (mix_tok[NH - 2][-1], mix_tok[NH - 1][-1],
               qdq_g[0], qdq_g[1], sf_g[0], sf_g[1])
    if "mix" in tap_d:
        ydma(tap_d["mix"], MIX[:], deps=mix_all)

    SQ = [arbf(0, 1024), arbf(1024, 1024)]
    EBB = arbf(2048, 2048)[0:64, :]
    GA = arf32(4096, 1024)
    GB = arf32(6144, 1024)
    VROW = arf32(8192, 1024)[0:1, :]
    RROW = arf32(10240, 1024)[0:1, :]

    ld_eb = pdma(EBB, eb_d[:], deps=p4_done, sem="ebl")

    sq_all = [None] * NH
    var_mm = None
    for h in range(NH):
        i = h % 2
        prev_sq = sq_all[h - 2] if h >= 2 else None
        sq_all[h] = act(SQ[i], MIX3[:, h, :], AF.Square,
                        deps=mix_tok[h] + p4_done + ((var_mm,) if h >= 2 else ()))
        for th in range(2):
            var_mm = mm(PB[th][0:1, :], INVD[:],
                        SQ[i][:, th * 512:(th + 1) * 512],
                        h == 0, h == NH - 1,
                        deps=(sq_all[h], ld_invd), bank=th)
    vcp = []
    for th in range(2):
        vcp.append(vop("tensor_copy", VROW[:, th * 512:(th + 1) * 512],
                       PB[th][0:1, :], deps=(var_mm,) + p4_done))
    bankg[0] = vcp[0]
    bankg[1] = vcp[1]
    veps = vop("tensor_scalar", RROW, VROW, RMS_EPS, None, ALU.add,
               deps=tuple(vcp) + p4_done)
    vln = act(VROW, RROW, AF.Ln, deps=(veps,))
    vlnm = vop("tensor_scalar", RROW, VROW, -0.5, None, ALU.mult, deps=(vln,))
    vrs = act(VROW, RROW, AF.Exp, deps=(vlnm,))
    rc_mm = None
    for tc in range(TC):
        rc_mm = mm(PB[2][:, tc:tc + 1], VROW[0:1, tc * 128:(tc + 1) * 128], ONE1[:],
                   tc == 0, tc == TC - 1, deps=(vrs, ld_one1), bank=2)
    rstd_tok = vop("tensor_copy", RSTDC[:], PB[2][:, 0:8], deps=(rc_mm,))
    bankg[2] = rstd_tok
    if "rstd" in tap_d:
        ydma(tap_d["rstd"], RSTDC[:], deps=(rstd_tok,))

    g_tok = [None] * CH
    ga_g = p4_done
    gb_g = p4_done
    b_last = None
    for fc in range(CH):
        pbase = 3 if fc % 2 == 0 else 5
        bmmt = [None, None]
        for th in range(2):
            bmmt[th] = mm(PB[pbase + th][:, :], EBB[:, fc * 128:(fc + 1) * 128],
                          MWT[:, th * 512:(th + 1) * 512], True, True,
                          deps=(ld_eb, mwt_tok[-1]), bank=pbase + th)
            b_last = bmmt[th]
        ga = vop("tensor_tensor", GA.rearrange("p (a t) -> p a t", a=TC),
                 XTL[:, :, fc, :],
                 RATIOB.rearrange("p (a t) -> p a t", a=TC), ALU.mult,
                 deps=(ratio_tok[0], ratio_tok[1]) + _tup(ga_g))
        gbs = []
        for th in range(2):
            gbs.append(vop("tensor_tensor", GB[:, th * 512:(th + 1) * 512],
                           GA[:, th * 512:(th + 1) * 512], PB[pbase + th][:, :],
                           ALU.add, deps=(ga, bmmt[th]) + _tup(gb_g)))
            bankg[pbase + th] = gbs[th]
        g_tok[fc] = act(VSBL[:, :, fc, :], GB, AF.Silu,
                        deps=tuple(gbs) + (mix_all[-1], p4_done[-1], p4_done[-2]))
        ga_g = gbs[1]
        gb_g = g_tok[fc]
    g_all = tuple(g_tok)

    gt_tok = [None] * TC
    for tc in range(TC):
        gt_tok[tc] = ydma(KSB[:, tc * 2048:(tc + 1) * 2048]
                          .rearrange("p (c f) -> p c f", c=CH),
                          VSB[:, tc * 2048:(tc + 1) * 2048]
                          .rearrange("p (c t) -> p c t", c=CH),
                          deps=g_all + (head_pre[NH - 1]["kt"], kd_g[1]),
                          transpose=True, sem="gtt")
    if "gt" in tap_d:
        ydma(tap_d["gt"], KSB[:], deps=tuple(gt_tok))

    ZT = arbf(12288, 1024)
    z_tok = [None] * CH
    zt_g = ()
    for oc in range(CH):
        zt = vop("tensor_tensor", ZT, MIX3[:, oc, :], GATE3[:, oc, :], ALU.mult,
                 deps=(sq_all[oc], gate_tok[(oc, 0)], gate_tok[(oc, 1)]) +
                 mix_tok[oc] + _tup(zt_g))
        z_tok[oc] = vop("tensor_scalar", MIX3[:, oc, :], ZT, NWC[:, oc:oc + 1],
                        None, ALU.mult, deps=(zt, ld_nw, sq_all[oc]))
        zt_g = z_tok[oc]

    if "z" in tap_d:
        ydma(tap_d["z"], MIX[:], deps=tuple(z_tok))

    WO = [arbf(0, 2048), arbf(2048, 2048)]
    WOF = arf32(4096, 2048)
    XH = [arf32(8192, 1024), arf32(8192, 1024)]
    YV = [arf32(10240, 1024), arf32(12288, 1024)]
    PO = [arf32(14336, 1024), arf32(14336, 1024)]

    wo_g = [(var_mm, sq_all[15], ld_eb, b_last),
            (var_mm, sq_all[15], ld_eb, b_last)]
    wof_g = (var_mm, sq_all[15]) + _tup(gb_g) + _tup(ga_g)
    xh_g = [(rstd_tok,), (rstd_tok,)]
    yv_g = [(vrs, rstd_tok, g_tok[15]), (vrs, rstd_tok, g_tok[15], zt_g)]
    po_g = [(vrs, rstd_tok, zt_g), (vrs, rstd_tok, zt_g)]
    out_st = []
    for fh in range(2):
        for tq in range(2):
            lastmm = {}
            wts = {}
            for op in range(8):
                wi = op % 2
                if wi == 0:
                    wts[op] = pdma(WO[0], wo8_d[fh, op], deps=_tup(wo_g[0]),
                                   sem="wol0")
                else:
                    wf = ydma(WOF, wo8_d[fh, op],
                              deps=_tup(wo_g[1]) + _tup(wof_g), sem="xh0")
                    wts[op] = vop("tensor_copy", WO[1], WOF, deps=(wf,))
                    wof_g = wts[op]
                for oi in range(2):
                    oc = 2 * op + oi
                    for ti in range(4):
                        tc = 4 * tq + ti
                        for q in range(2):
                            bank = ti * 2 + q
                            first = (oc == 0)
                            lastmm[(ti, q)] = mm(
                                PB[bank][:, :],
                                MIX3[:, oc, tc * 128:(tc + 1) * 128],
                                WO[wi][:, oi * 1024 + q * 512:oi * 1024 + (q + 1) * 512],
                                first, oc == CH - 1,
                                deps=(wts[op], z_tok[oc]), bank=bank)
                wo_g[wi] = lastmm[(3, 1)]
            for ti in range(4):
                tc = 4 * tq + ti
                ip = ti % 2
                xh = ydma(XH[ip], xs_d[tc * 128:(tc + 1) * 128,
                                       fh * 1024:(fh + 1) * 1024],
                          deps=_tup(xh_g[0]) + _tup(xh_g[1]), sem="xh1")
                yv = []
                for q in range(2):
                    yv.append(act(YV[ip][:, q * 512:(q + 1) * 512],
                                  PB[ti * 2 + q][:, :], AF.Copy,
                                  deps=(lastmm[(ti, q)], rstd_tok) + _tup(yv_g[ip]),
                                  scale=RSTDC[:, tc:tc + 1]))
                    bankg[ti * 2 + q] = yv[q]
                if fh == 0 and tq == 0 and ti == 0 and "yv00" in tap_d:
                    ydma(tap_d["yv00"], YV[ip], deps=tuple(yv))
                pm = vop("tensor_tensor", PO[ip], YV[ip],
                         KSB[:, tc * 2048 + fh * 1024:tc * 2048 + fh * 1024 + 1024],
                         ALU.mult,
                         deps=tuple(yv) + (gt_tok[TC - 1],) + _tup(po_g[ip]))
                po = vop("tensor_tensor", YV[ip], PO[ip], XH[ip], ALU.add,
                         deps=(pm, xh))
                po_g[ip] = po
                st = ydma(out_d[tc * 128:(tc + 1) * 128, fh * 1024:(fh + 1) * 1024],
                          YV[ip], deps=(po,), sem="ost%d" % ip)
                out_st.append(st)
                yv_g[ip] = (po, st)
                xh_g[ip] = (po,)

    final_counts = {k: R.counts[k] for k in R.counts if R.counts[k] > 0}

    for engname, ops in R.stream.items():
        for fn, deps, sem, n in ops:
            for dsem, dval in deps:
                assert dval <= R.counts[dsem], (engname, dsem, dval, R.counts[dsem])

    def make_body(engname):
        def body(eng):
            last = {}
            for fn, deps, sem, n in R.stream[engname]:
                for dsem, dval in deps:
                    if last.get(dsem, 0) < dval:
                        eng.wait_ge(sems[dsem], dval)
                        last[dsem] = dval
                inst = fn(eng)
                inst.then_inc(sems[sem], n)
            if engname == "sync":
                for kk in ("ost0", "ost1", "y"):
                    if final_counts.get(kk):
                        eng.wait_ge(sems[kk], final_counts[kk])
        return body

    with nc.Block() as block:
        block.sync(make_body("sync"))
        block.gpsimd(make_body("gpsimd"))
        block.scalar(make_body("scalar"))
        block.vector(make_body("vector"))
        block.tensor(make_body("tensor"))

    for g in reversed(psg):
        g.__exit__(None, None, None)
    for g in reversed(sbg):
        g.__exit__(None, None, None)
    return nc


def _shared_arrays(wqkv, wgate, wout, wrout, rbias, ebias, normw, slope):
    f32 = np.float32
    rows_k = (384 * np.arange(NH)[:, None] + 128 + np.arange(HD)[None, :]).reshape(-1)
    rows_v = (384 * np.arange(NH)[:, None] + 256 + np.arange(HD)[None, :]).reshape(-1)
    rows_q = (384 * np.arange(NH)[:, None] + np.arange(HD)[None, :]).reshape(-1)
    wsel = wqkv[np.concatenate([rows_k, rows_v])]
    wkv8 = np.ascontiguousarray(
        wsel.reshape(8, 512, 16, 128).transpose(0, 2, 3, 1))
    wq16 = np.ascontiguousarray(
        wqkv[rows_q].reshape(16, 128, 16, 128).transpose(0, 2, 3, 1))
    wg16 = np.ascontiguousarray(
        wgate.reshape(16, 128, 16, 128).transpose(0, 2, 3, 1))
    woT = np.ascontiguousarray(wout.T)
    wo8 = np.ascontiguousarray(
        woT.reshape(8, 2, 128, 2, 1024).transpose(3, 0, 2, 1, 4)
        .reshape(2, 8, 128, 2048))
    wr16 = np.ascontiguousarray(wrout.T.reshape(16, 128, 64))
    rb = np.ascontiguousarray(rbias.reshape(64, 1)).astype(f32)
    nw = np.ascontiguousarray(normw.reshape(16, 128).T).astype(f32)
    slopes = np.ascontiguousarray(slope.reshape(1, 16)).astype(f32)

    m = np.arange(1, 257, dtype=f32)
    identb = np.eye(128, dtype=NPBF)
    identf = np.eye(128, dtype=f32)
    iotam = m.reshape(1, 256).copy()
    jneg = np.tile(np.arange(4, dtype=f32) * (-256.0), 16).reshape(1, 64)
    iotanm = np.stack([np.arange(1, 129, dtype=f32) - 256.0,
                       np.arange(129, 257, dtype=f32) - 256.0], axis=1)
    sel8 = np.zeros((8, 1024), dtype=f32)
    for k in range(8):
        sel8[k, k * 128:(k + 1) * 128] = 1.0
    invd = np.full((128, 1), 1.0 / H, dtype=NPBF)
    p = np.arange(128, dtype=f32)
    idxt = np.zeros((128, 512), dtype=f32)
    for jc in range(2):
        idxt[:, jc * 256:(jc + 1) * 256] = (m[None, :] - 1.0) - (128 * jc + p[:, None])
    maskt = (idxt >= 0).astype(f32)
    onesr = np.ones((1, 128), dtype=f32)
    one1 = np.ones((1, 1), dtype=f32)
    return dict(wkv8=wkv8.astype(f32), wq16=wq16.astype(f32), wg16=wg16.astype(f32),
                wo8=wo8.astype(f32), wr16=wr16.astype(f32), rb=rb,
                eb=np.ascontiguousarray(ebias).astype(f32), nw=nw, slopes=slopes,
                identb=identb, identf=identf, iotam=iotam, jneg=jneg.astype(f32),
                iotanm=iotanm.astype(f32), sel8=sel8, invd=invd,
                idxt=idxt, maskt=maskt, onesr=onesr, one1=one1,
                onesrb=np.ones((1, 128), dtype=NPBF))


def make_in_maps(inputs):
    x = np.asarray(inputs["L0_attn_hidden_states"], dtype=np.float32)
    shared = _shared_arrays(
        np.asarray(inputs["L0_attn_w_qkv"], np.float32),
        np.asarray(inputs["L0_attn_w_gate"], np.float32),
        np.asarray(inputs["L0_attn_w_out"], np.float32),
        np.asarray(inputs["L0_ffn_router_weight"], np.float32),
        np.asarray(inputs["L0_ffn_router_bias"], np.float32),
        np.asarray(inputs["L0_ffn_expert_bias"], np.float32),
        np.asarray(inputs["L0_attn_norm_w"], np.float32),
        np.asarray(inputs["L0_attn_slope_rate"], np.float32),
    )
    in_maps = []
    for core in range(8):
        b, half = core >> 1, core & 1
        mdict = dict(shared)
        mdict["xs"] = np.ascontiguousarray(x[b, half * T:(half + 1) * T, :])
        send = 1.0 if half == 0 else 0.0
        recv = 0.0 if half == 0 else 1.0
        mdict["masks"] = np.tile(np.array([[send, recv]], np.float32), (128, 1))
        in_maps.append(mdict)
    return in_maps


def kernel(**inputs):
    in_maps = make_in_maps(inputs)
    nc = build()
    res = run_bass_kernel_spmd(nc, in_maps, core_ids=list(range(8)))
    out = np.empty((4, 2048, H), dtype=np.float32)
    for core in range(8):
        b, half = core >> 1, core & 1
        out[b, half * T:(half + 1) * T, :] = res.results[core]["out"]
    return out


# version 07
# speedup vs baseline: 1.1380x; 1.1380x over previous
import numpy as np
import ml_dtypes
import concourse.bass as bass
import concourse.mybir as mybir
from concourse.bass_utils import run_bass_kernel_spmd

F32 = mybir.dt.float32
BF16 = mybir.dt.bfloat16
AF = mybir.ActivationFunctionType
ALU = mybir.AluOpType
AX = mybir.AxisListType
NPBF = ml_dtypes.bfloat16

H = 2048
NH = 16
HD = 128
T = 1024
TC = 8
CH = 16
NBLK = 4
NE = 64
LN_EPS = 1e-5
RMS_EPS = 1e-6

ENG2SEM = {"sync": "y", "gpsimd": "p", "scalar": "s", "vector": "v", "tensor": "t"}



def _tup(x):
    if x is None:
        return ()
    if isinstance(x, tuple) and len(x) == 2 and isinstance(x[0], str) \
            and isinstance(x[1], int):
        return (x,)
    return tuple(t for t in x if t is not None)

class Rec:
    def __init__(self):
        self.stream = {e: [] for e in ENG2SEM}
        self.counts = {k: 0 for k in ("y", "p", "s", "v", "t", "cc")}

    def o(self, eng, fn, deps=(), sem=None, n=1):
        if sem is None:
            sem = ENG2SEM[eng]
        self.counts[sem] = self.counts.get(sem, 0) + n
        self.stream[eng].append((fn, tuple(d for d in deps if d is not None), sem, n))
        return (sem, self.counts[sem])


def build(taps=()):
    nc = bass.Bass("TRN2", target_bir_lowering=False, dynamic_dma_scratch_size=16384)

    def din(name, shape, dt=F32):
        return nc.dram_tensor(name, shape, dt, kind="ExternalInput").ap()

    xs_d = din("xs", [T, H])
    wkv8_d = din("wkv8", [8, 16, 128, 512])
    wq16_d = din("wq16", [16, 16, 128, 128])
    wg16_d = din("wg16", [16, 16, 128, 128])
    wo8_d = din("wo8", [2, 8, 128, 2048])
    wr16_d = din("wr16", [16, 128, 64])
    rb_d = din("rb", [64, 1])
    eb_d = din("eb", [64, H])
    nw_d = din("nw", [128, 16])
    slopes_d = din("slopes", [1, 16])
    masks_d = din("masks", [128, 2])
    identb_d = din("identb", [128, 128], BF16)
    identf_d = din("identf", [128, 128])
    iotam_d = din("iotam", [1, 256])
    jneg_d = din("jneg", [1, 64])
    iotanm_d = din("iotanm", [128, 2])
    sel8_d = din("sel8", [8, 1024])
    invd_d = din("invd", [128, 1], BF16)
    idxt_d = din("idxt", [128, 512])
    maskt_d = din("maskt", [128, 512])
    onesr_d = din("onesr", [1, 128])
    onesrb_d = din("onesrb", [1, 128], BF16)
    one1_d = din("one1", [1, 1])

    out_d = nc.dram_tensor("out", [T, H], F32, kind="ExternalOutput").ap()
    tap_d = {}
    for tp, shape, dt in (
        ("xt", [128, CH * T], BF16), ("ksb", [128, TC * H], BF16),
        ("vsb", [128, TC * H], BF16), ("mix", [128, NH * T], BF16),
        ("gate", [128, CH * T], BF16), ("lg", [64, T], F32),
        ("gt", [128, TC * H], BF16), ("ratio", [128, T], BF16),
        ("mwt", [64, T], BF16), ("rstd", [128, 8], F32),
        ("stb", [NH, NBLK, 128, 128], F32), ("ccout", [2 * NH * HD, 128], F32),
        ("qbf0", [128, T], BF16), ("qbf1", [128, T], BF16),
        ("kt0", [128, T], BF16), ("kt1", [128, T], BF16),
        ("ddh1", [128, 512], BF16), ("qdh1", [128, 256], BF16),
        ("sf1", [128, 128], BF16), ("qkp1", [128, 512], BF16),
        ("stbh1", [128, 4, 128], F32),
        ("z", [128, NH * T], BF16), ("yv00", [128, 1024], F32),
    ):
        if tp in taps:
            tap_d[tp] = nc.dram_tensor("tap_" + tp, shape, dt, kind="ExternalOutput").ap()

    xlo_d = nc.dram_tensor("xlo_i", [T, H], BF16, kind="Internal").ap()
    dd_d = nc.dram_tensor("dd_i", [NH, 128, 512], BF16, kind="Internal").ap()
    qd_d = nc.dram_tensor("qd_i", [NH, 128, 256], BF16, kind="Internal").ap()
    stb_d = nc.dram_tensor("stb_i", [NH, NBLK, 128, 128], F32, kind="Internal").ap()
    ccin_d = nc.dram_tensor("ccin_i", [NH * HD, 128], F32, kind="Internal").ap()
    ccout_d = nc.dram_tensor("ccout_i", [2 * NH * HD, 128], F32, kind="Internal").ap()

    sbg = []

    def sb(name, shape, dt, **kw):
        g = nc.sbuf_tensor(name, shape, dt, **kw)
        t = g.__enter__()
        sbg.append(g)
        return t

    XT = sb("XT", [128, CH * T], BF16, is_transpose_dest=True)
    KSB = sb("KSB", [128, TC * H], BF16, is_transpose_dest=True)
    VSB = sb("VSB", [128, TC * H], BF16)
    MIX = sb("MIX", [128, NH * T], BF16)
    GATE = sb("GATE", [128, CH * T], BF16)
    WSM = [sb("WSM0", [128, 2048], BF16), sb("WSM1", [128, 2048], BF16)]
    AR = sb("AR", [128, 16384], BF16, is_transpose_dest=True)
    IDB = sb("IDB", [128, 128], BF16)
    IDF = sb("IDF", [128, 128], F32)
    ONESR = sb("ONESR", [1, 128], F32)
    ONESRB = sb("ONESRB", [1, 128], BF16)
    ONE1 = sb("ONE1", [1, 1], F32)
    JNEG = sb("JNEG", [1, 64], F32)
    IOTANM = sb("IOTANM", [128, 2], F32)
    INVD = sb("INVD", [128, 1], BF16)
    SLOPES = sb("SLOPES", [1, 16], F32)
    SNEGR = sb("SNEGR", [1, 16], F32)
    SBC = sb("SBC", [128, 16], F32)
    SNEGBC = sb("SNEGBC", [128, 16], F32)
    BDBC = sb("BDBC", [128, 16], F32)
    BDPBC = sb("BDPBC", [128, 64], F32)
    KDCOL = sb("KDCOL", [128, 32], F32)
    NWC = sb("NWC", [128, 16], F32)
    MASKSB = sb("MASKSB", [128, 2], F32)
    RBC = sb("RBC", [64, 1], F32)
    MWT = sb("MWT", [64, T], BF16)
    RATIOB = sb("RATIOB", [128, T], BF16)
    RAT8 = sb("RAT8", [128, 8], F32)
    RSTDC = sb("RSTDC", [128, 8], F32)
    STATS = sb("STATS", [128, 24], F32)
    TOPC = sb("TOPC", [128, 24], F32)
    ROWS = sb("ROWSC", [1, 144], F32)

    psg = [nc.psum_tensor("pb%d" % i, [128, 512], F32) for i in range(8)]
    PB = [g.__enter__() for g in psg]

    semnames = ["y", "p", "s", "v", "t", "cc", "cn", "xs0", "xs1", "xt0", "xt1",
                "xlo0", "xlo1", "l12", "wr", "sel", "ixm",
                "wkv0", "wkv1", "wkv2", "wsm0", "wsm1",
                "ddst", "qdst", "stbst", "sdst",
                "kt0", "kt1", "ddl0", "ddl1", "qdl0", "qdl1",
                "stbl0", "stbl1", "kvr0", "kvr1", "ebl",
                "gtt", "xh0", "xh1", "wol0", "wol1", "ost0", "ost1"]
    sems = {k: nc.alloc_semaphore("sem_" + k) for k in semnames}
    R = Rec()

    def arf32(lo, n):
        return AR[:, lo:lo + 2 * n].bitcast(F32)

    def arbf(lo, n):
        return AR[:, lo:lo + n]

    XTL = XT.rearrange("p (a c t) -> p a c t", a=TC, c=CH)
    KSBH = KSB.rearrange("p (h a d) -> p h a d", h=NH, a=TC)
    VSBL = VSB.rearrange("p (a c f) -> p a c f", a=TC, c=CH)
    VSB3 = VSB.rearrange("p (a o) -> p a o", a=TC)
    MIX3 = MIX.rearrange("p (h t) -> p h t", h=NH)
    GATE3 = GATE.rearrange("p (c t) -> p c t", c=CH)

    def ydma(out, in_, deps=(), transpose=False, sem=None):
        if transpose:
            return R.o("sync", lambda e, o=out, i=in_: e.dma_start(out=o, in_=i, transpose=True), deps, n=16, sem=sem)
        return R.o("sync", lambda e, o=out, i=in_: e.dma_start(out=o, in_=i), deps, n=16, sem=sem)

    def pdma(out, in_, deps=(), sem=None):
        return R.o("gpsimd", lambda e, o=out, i=in_: e.dma_start(out=o, in_=i), deps, n=16, sem=sem)

    bankg = [None] * 8

    def mm(bank_or_out, lhsT, rhs, start, stop, deps=(), bank=None):
        out = bank_or_out
        if bank is not None and start:
            deps = tuple(deps) + (bankg[bank],)
        return R.o("tensor", lambda e, o=out, l=lhsT, r=rhs, st=start, sp=stop:
                   e.matmul(o, l, r, start=st, stop=sp), deps)

    def tr(out, in_, ident, deps=(), bank=None):
        if bank is not None:
            deps = tuple(deps) + (bankg[bank],)
        return R.o("tensor", lambda e, o=out, i=in_, d=ident: e.transpose(o, i, d), deps)

    def act(out, in_, func, deps=(), bias=0.0, scale=1.0, accum_out=None):
        if func is AF.Copy and not (isinstance(bias, float) and isinstance(scale, float)):
            func = AF.Identity
        return R.o("scalar", lambda e, o=out, i=in_, f=func, b=bias, sc=scale, a=accum_out:
                   e.activation(o, i, f, bias=b, scale=sc, accum_out=a), deps)

    def vop(name, *args, deps=()):
        return R.o("vector", lambda e, nm=name, aa=args: getattr(e, nm)(*aa), deps)

    for dst, srcd in ((IDB[:], identb_d[:]), (IDF[:], identf_d[:]),
                      (ONESR[:], onesr_d[:]), (ONE1[:], one1_d[:]),
                      (JNEG[:], jneg_d[:]),
                      (IOTANM[:], iotanm_d[:]), (INVD[:], invd_d[:]),
                      (SLOPES[:], slopes_d[:]), (NWC[:], nw_d[:]),
                      (ONESRB[:], onesrb_d[:]),
                      (MASKSB[:], masks_d[:]), (RBC[:], rb_d[:])):
        CT = ydma(dst, srcd, sem="cn")
    ld_idb = ld_idf = ld_onesr = ld_one1 = ld_iotam = ld_jneg = CT
    ld_iotanm = ld_invd = ld_slopes = ld_nw = ld_masks = ld_rb = CT

    XSL = [arf32(4096 * i, 2048) for i in range(4)]
    XNB = [MIX[:, 0:2048], MIX[:, 2048:4096]]
    SQD = GATE[:, 2048:4096]
    XLO1 = GATE[:, 0:2048]

    xt_tok = [None] * TC
    xlo_st = [None] * TC
    ops = {tc: {} for tc in range(TC)}

    def p0_g1(tc):
        i = tc % 4
        c0 = 8 * (tc % 3)
        o = ops[tc]
        pv = ops.get(tc - 4, {})
        o["xld"] = ydma(XSL[i], xs_d[tc * 128:(tc + 1) * 128, :],
                        deps=(pv.get("x1"), pv.get("xlo")), sem="xs%d" % (i % 2))
        o["sum"] = vop("reduce_sum", STATS[:, c0:c0 + 1], XSL[i], AX.X,
                       deps=(o["xld"],))
        sqg = ops.get(tc - 1, {}).get("sq2")
        o["sq1"] = act(SQD[:, 0:1024], XSL[i][:, 0:1024], AF.Square,
                       deps=(o["xld"], sqg),
                       accum_out=STATS[:, c0 + 1:c0 + 2])
        o["sq2"] = act(SQD[:, 1024:2048], XSL[i][:, 1024:2048], AF.Square,
                       deps=(o["xld"],),
                       accum_out=STATS[:, c0 + 2:c0 + 3])

    def p0_g2(tc):
        i = tc % 4
        j = tc % 2
        c0 = 8 * (tc % 3)
        o = ops[tc]
        o["mean"] = vop("tensor_scalar", STATS[:, c0 + 3:c0 + 4],
                        STATS[:, c0:c0 + 1], 1.0 / H, None, ALU.mult,
                        deps=(o["sum"],))
        o["ex2"] = vop("tensor_scalar", STATS[:, c0 + 4:c0 + 5],
                       STATS[:, c0 + 1:c0 + 2], STATS[:, c0 + 2:c0 + 3], 1.0 / H,
                       ALU.add, ALU.mult, deps=(o["sq1"], o["sq2"]))
        o["m2"] = vop("tensor_tensor", STATS[:, c0 + 5:c0 + 6],
                      STATS[:, c0 + 3:c0 + 4], STATS[:, c0 + 3:c0 + 4],
                      ALU.mult, deps=(o["mean"],))
        o["var"] = vop("tensor_scalar", STATS[:, c0 + 6:c0 + 7],
                       STATS[:, c0 + 4:c0 + 5], STATS[:, c0 + 5:c0 + 6], LN_EPS,
                       ALU.subtract, ALU.add, deps=(o["ex2"], o["m2"]))
        o["ln"] = act(STATS[:, c0 + 7:c0 + 8], STATS[:, c0 + 6:c0 + 7],
                      AF.Ln, deps=(o["var"],))
        o["lnm"] = vop("tensor_scalar", STATS[:, c0 + 6:c0 + 7],
                       STATS[:, c0 + 7:c0 + 8], -0.5, None, ALU.mult,
                       deps=(o["ln"],))
        o["rstd"] = act(STATS[:, c0 + 7:c0 + 8], STATS[:, c0 + 6:c0 + 7],
                        AF.Exp, deps=(o["lnm"],))
        o["nm"] = vop("scalar_tensor_tensor", STATS[:, c0 + 5:c0 + 6],
                      STATS[:, c0 + 3:c0 + 4], -1.0,
                      STATS[:, c0 + 7:c0 + 8], ALU.mult, ALU.mult,
                      deps=(o["rstd"], o["m2"]))
        o["xn"] = act(XNB[j], XSL[i], AF.Copy,
                      deps=(o["rstd"], o["nm"], o["sq1"], o["sq2"],
                            ops.get(tc - 2, {}).get("xt"),
                            ops.get(tc - 2, {}).get("x1")),
                      bias=STATS[:, c0 + 5:c0 + 6], scale=STATS[:, c0 + 7:c0 + 8])
        o["xt"] = ydma(XTL[:, tc], XNB[j].rearrange("t (c p) -> t c p", c=CH),
                       deps=(o["xn"],), transpose=True, sem="xt%d" % j)
        xt_tok[tc] = o["xt"]

    def p0_g3(tc):
        i = tc % 4
        j = tc % 2
        c0 = 8 * (tc % 3)
        o = ops[tc]
        o["x1"] = vop("scalar_tensor_tensor", XLO1, XSL[i],
                      STATS[:, c0 + 7:c0 + 8], XNB[j], ALU.mult, ALU.subtract,
                      deps=(o["xn"], ops.get(tc - 1, {}).get("x2")))
        XLO2i = arbf(4096 * i, 2048)
        o["x2"] = vop("tensor_scalar", XLO2i, XLO1, STATS[:, c0 + 5:c0 + 6],
                      None, ALU.add, deps=(o["x1"],))
        o["xlo"] = pdma(xlo_d[tc * 128:(tc + 1) * 128, :], XLO2i,
                        deps=(o["x2"],), sem="xlo%d" % (i % 2))
        xlo_st[tc] = o["xlo"]

    for k in range(TC + 4):
        if k - 4 >= 0:
            p0_g3(k - 4)
        if 0 <= k - 2 < TC:
            p0_g2(k - 2)
        if k < TC:
            p0_g1(k)
    xt_all = tuple(xt_tok)
    xsl_g = [(ops[6]["x1"], ops[6]["xlo"]), (ops[7]["x1"], ops[7]["xlo"])]
    xnb_g = [(ops[6]["xt"], ops[6]["x1"]), (ops[7]["xt"], ops[7]["x1"])]
    xlo1_g = ops[7]["x2"]
    sq_last = ops[7]["sq2"]
    xn_last = (ops[6]["xt"], ops[7]["xt"], ops[6]["x1"], ops[7]["x1"])
    if "xt" in tap_d:
        ydma(tap_d["xt"], XT[:], deps=xt_all)

    sneg_tok = vop("tensor_scalar", SNEGR[:], SLOPES[:], -1.0, None, ALU.mult,
                   deps=(ld_slopes,))
    bdr_u = vop("tensor_scalar", ROWS[:, 80:96], SLOPES[:], -256.0, None,
                ALU.mult, deps=(ld_slopes,))
    bdrow = act(ROWS[:, 0:16], ROWS[:, 80:96], AF.Exp, deps=(bdr_u,))
    bdp_u = vop("tensor_tensor",
                ROWS[:, 16:80].rearrange("a (e j) -> a e j", e=16),
                JNEG[:].rearrange("a (e j) -> a e j", e=16),
                SLOPES[0:1, :].rearrange("a (e o) -> a e o", o=1).broadcast_to([1, 16, 4]),
                ALU.mult, deps=(ld_jneg, ld_slopes))
    bdprow = act(ROWS[:, 80:144], ROWS[:, 16:80], AF.Exp, deps=(bdp_u, bdrow))
    b1 = mm(PB[4][:, 0:16], ONESR[:], SLOPES[:], True, False,
            deps=(ld_onesr, ld_slopes), bank=4)
    b2 = mm(PB[4][:, 16:32], ONESR[:], ROWS[:, 0:16], False, False, deps=(bdrow,))
    b3 = mm(PB[4][:, 32:96], ONESR[:], ROWS[:, 80:144], False, True, deps=(bdprow,))
    e_sbc = vop("tensor_copy", SBC[:], PB[4][:, 0:16], deps=(b3,))
    e_bd = vop("tensor_copy", BDBC[:], PB[4][:, 16:32], deps=(b3,))
    e_bdp = vop("tensor_copy", BDPBC[:], PB[4][:, 32:96], deps=(b3,))
    bankg[4] = e_bdp
    e_sneg = vop("tensor_scalar", SNEGBC[:], SBC[:], -1.0, None, ALU.mult,
                 deps=(e_sbc,))
    kd_prev = None
    for h in range(NH):
        cs = 16 + (h % 2) * 2
        u = vop("tensor_scalar", STATS[:, cs:cs + 2], IOTANM[:],
                SBC[:, h:h + 1], None, ALU.mult,
                deps=(ld_iotanm, e_sbc, kd_prev))
        kd_prev = act(KDCOL[:, 2 * h:2 * h + 2], STATS[:, cs:cs + 2],
                      AF.Exp, deps=(u,))
    kd_all = kd_prev

    DDA = MIX[:, 4096:5120].bitcast(F32)
    DDB = MIX[:, 5120:6144].bitcast(F32)
    DDC = MIX[:, 6144:6656]
    IDXA = MIX[:, 6656:7680].bitcast(F32)
    MKA = MIX[:, 7680:8704].bitcast(F32)
    IOTA_A = MIX[:, 8704:9216].bitcast(F32)[0:1, :]
    QR16 = MIX[:, 9216:13312]

    ld_idxt = ydma(IDXA, idxt_d[:], deps=(), sem="ixm")
    ld_maskt = ydma(MKA, maskt_d[:], deps=(), sem="ixm")
    ld_iotam2 = ydma(IOTA_A, iotam_d[:], deps=(), sem="ixm")

    dd_st = [None] * NH
    qr_tok = [None] * NH
    dda_g = None
    ddb_g = None
    ddc_g = None
    for h in range(NH):
        u1 = vop("tensor_scalar", DDA, IDXA, SNEGBC[:, h:h + 1], None, ALU.mult,
                 deps=(ld_idxt, e_sneg, dda_g))
        u2 = vop("tensor_scalar", DDB, DDA, 0.0, None, ALU.min,
                 deps=(u1, ddb_g))
        u3 = act(DDA, DDB, AF.Exp, deps=(u2,))
        dd = vop("tensor_tensor", DDC, DDA, MKA, ALU.mult,
                 deps=(u3, ld_maskt, ddc_g))
        dd_st[h] = ydma(dd_d[h], DDC, deps=(dd,), sem="ddst")
        dda_g = dd
        ddb_g = u3
        ddc_g = dd_st[h]
        qr_tok[h] = act(QR16[0:1, h * 256:(h + 1) * 256], IOTA_A, AF.Exp,
                        deps=(ld_iotam2, sneg_tok), scale=SNEGR[0:1, h:h + 1])
    dd_all = dd_st[NH - 1]
    qr_all = qr_tok[NH - 1]

    WKV = [arbf(0, 8192).rearrange("p (c j) -> p c j", c=CH),
           arbf(8192, 8192).rearrange("p (c j) -> p c j", c=CH)]
    slot_g = [(ops[4]["x1"], ops[4]["xlo"], ops[5]["x1"], ops[5]["xlo"]),
              (xsl_g[0], xsl_g[1], xlo1_g)]
    wkv_ld = [None] * 8

    def wkv_load(ob):
        g = ()
        for x_ in _tup(slot_g[ob % 2]):
            g += _tup(x_)
        return pdma(WKV[ob % 2], wkv8_d[ob].rearrange("c p j -> p c j"),
                    deps=g, sem="wkv%d" % (ob % 2))

    for ob in range(2):
        wkv_ld[ob] = wkv_load(ob)
    ev_tok = {}
    for ob in range(8):
        obk = ob if ob < 4 else ob - 4
        for tg in range(2):
            lastmm = {}
            for c in range(CH):
                for k in range(4):
                    tc = tg * 4 + k
                    bank = tg * 4 + k
                    first = (c == 0)
                    deps = [wkv_ld[ob]]
                    if first:
                        deps += [xt_tok[tg * 4 + kk] for kk in range(4)]
                    lastmm[tc] = mm(PB[bank][:, :],
                                    XTL[:, tc, c, :],
                                    WKV[ob % 2][:, c, :],
                                    first, (c == CH - 1),
                                    deps=deps, bank=bank)
            for k in range(4):
                tc = tg * 4 + k
                if ob < 4:
                    dst = KSBH[:, 4 * obk:4 * obk + 4, tc, :]
                else:
                    dst = VSB3[:, tc, 512 * obk:512 * obk + 512]
                ev_tok[(ob, tc)] = act(dst, PB[tg * 4 + k][:, :], AF.Silu,
                                       deps=(lastmm[tc],))
                bankg[tg * 4 + k] = ev_tok[(ob, tc)]
        slot_g[ob % 2] = lastmm[7]
        if ob + 2 < 8:
            wkv_ld[ob + 2] = wkv_load(ob + 2)
    kv_all = tuple(ev_tok[(ob, tc)] for ob in range(8) for tc in range(TC))
    k_done = tuple(ev_tok[(ob, tc)] for ob in range(4) for tc in range(TC))
    p1_done = kv_all[-1]
    if "ksb" in tap_d:
        ydma(tap_d["ksb"], KSB[:], deps=k_done)
    if "vsb" in tap_d:
        ydma(tap_d["vsb"], VSB[:], deps=kv_all)

    arena_g0 = _tup(slot_g[0])
    arena_g1 = _tup(slot_g[1])

    XLOT = GATE[:, 2048:10240].rearrange("p (a c t) -> p a c t", a=4, c=CH)
    WRF = GATE[:, 10240:12288].bitcast(F32)
    WRHI = GATE[:, 12288:13312]
    WRLO = GATE[:, 13312:14336]
    SEL8A = GATE[:, 14336:16384].bitcast(F32)[0:8, :]
    WRF3 = WRF.rearrange("p (c j) -> p c j", c=CH)
    WRHI3 = WRHI.rearrange("p (c j) -> p c j", c=CH)
    WRLO3 = WRLO.rearrange("p (c j) -> p c j", c=CH)

    ld_wrf = ydma(WRF3, wr16_d.rearrange("c p j -> p c j"), deps=(), sem="wr")
    wr_hi = act(WRHI, WRF, AF.Copy, deps=(ld_wrf,))
    wr_lo = vop("tensor_tensor", WRLO, WRF, WRHI, ALU.subtract, deps=(wr_hi,))
    ld_sel8 = ydma(SEL8A, sel8_d[:], deps=(), sem="sel")

    lg_mm_last = [None, None]
    prev_l2 = None
    for hf in range(2):
        tok_tr = []
        for tci in range(4):
            tc = hf * 4 + tci
            l2 = ydma(
                XLOT[:, tci],
                xlo_d[tc * 128:(tc + 1) * 128, :].rearrange("t (c p) -> t c p", c=CH),
                deps=(xlo_st[tc], prev_l2) +
                ((sq_last,) if hf == 0 and tci == 0 else ()) +
                ((lg_mm_last[0],) if hf == 1 else ()),
                transpose=True, sem="l12")
            prev_l2 = l2
            tok_tr.append(l2)
        first = True
        lastmm = None
        for c in range(CH):
            xhi = XTL[:, 4 * hf:4 * hf + 4, c, :]
            xlo = XLOT[:, :, c, :]
            whi = WRHI3[:, c, :]
            wlo = WRLO3[:, c, :]
            d0 = (wr_hi,) + xt_all + tuple(tok_tr) if first else (wr_hi,)
            lastmm = mm(PB[5 + hf][0:64, :], whi, xhi, first, False, deps=d0,
                        bank=5 + hf)
            first = False
            lastmm = mm(PB[5 + hf][0:64, :], wlo, xhi, False, False, deps=(wr_lo,))
            lastmm = mm(PB[5 + hf][0:64, :], whi, xlo, False, (c == CH - 1),
                        deps=(tok_tr[-1],))
        lg_mm_last[hf] = lastmm

    LGSB = arf32(0, 1024)[0:64, :]
    lg_tok = []
    for hf in range(2):
        lg_tok.append(act(LGSB[:, hf * 512:(hf + 1) * 512], PB[5 + hf][0:64, :],
                          AF.Copy, deps=(lg_mm_last[hf], ld_rb, wr_lo), bias=RBC[:]))
    bankg[5] = lg_tok[0]
    bankg[6] = lg_tok[1]
    if "lg" in tap_d:
        ydma(tap_d["lg"], LGSB, deps=tuple(lg_tok))

    E_t = arf32(2048, 64)
    MB_t = arbf(2176, 64)
    MS_t = arf32(2240, 64)
    MS2_t = arf32(2368, 64)
    mwt_tok = [None] * TC
    rat_tok = [None] * TC
    prev_lcp = None
    prev_ee = None
    prev_mwb = None
    for tc in range(TC):
        hf = tc // 4
        ltr = tr(PB[5][:, 0:64], LGSB[:, tc * 128:(tc + 1) * 128], IDF[0:64, 0:64],
                 deps=(lg_tok[hf], ld_idf, prev_lcp), bank=5)
        lcp = vop("tensor_copy", MS_t, PB[5][:, 0:64],
                  deps=(ltr, prev_mwb) + (arena_g0 if tc == 0 else ()))
        bankg[5] = lcp
        nmx = vop("reduce_max", TOPC[:, 0:1], MS_t, AX.X, None, None, True,
                  deps=(lcp,))
        ee = act(E_t, MS_t, AF.Exp,
                 deps=(nmx, prev_ee) + (arena_g0 if tc == 0 else ()),
                 bias=TOPC[:, 0:1])
        t8 = vop("max", TOPC[:, 2:10], E_t, deps=(ee,))
        den = vop("reduce_sum", TOPC[:, 10:11], E_t, AX.X, deps=(ee,))
        rz = vop("reciprocal", TOPC[:, 11:12], TOPC[:, 10:11], deps=(den,))
        s8 = vop("reduce_sum", TOPC[:, 12:13], TOPC[:, 2:10], AX.X, deps=(t8,))
        rat_tok[tc] = vop("tensor_tensor", RAT8[:, tc:tc + 1], TOPC[:, 12:13],
                          TOPC[:, 11:12], ALU.mult, deps=(s8, rz))
        msk = vop("tensor_scalar", MS_t, E_t, TOPC[:, 9:10], None, ALU.is_ge,
                  deps=(t8, lcp))
        mw1 = vop("tensor_tensor", MS2_t, MS_t, E_t, ALU.mult,
                  deps=(msk, ee) + (arena_g0 if tc == 0 else ()))
        mwb = vop("tensor_scalar", MB_t, MS2_t, TOPC[:, 11:12], None, ALU.mult,
                  deps=(mw1, rz, prev_mwb) + (arena_g0 if tc == 0 else ()))
        mtr = tr(PB[6][0:64, 0:256].bitcast(BF16)[:, 0:128], MB_t, IDB[:],
                 deps=(mwb, ld_idb, mwt_tok[tc - 1] if tc else None), bank=6)
        mwt_tok[tc] = vop("tensor_copy", MWT[:, tc * 128:(tc + 1) * 128],
                          PB[6][0:64, 0:256].bitcast(BF16)[:, 0:128], deps=(mtr,))
        bankg[6] = mwt_tok[tc]
        prev_lcp = lcp
        prev_ee = mw1
        prev_mwb = mwb
    rtr = tr(PB[5][0:8, 0:128], RAT8[:], IDF[:],
             deps=tuple(rat_tok) + (ld_idf, prev_lcp), bank=5)
    RRS = MIX[:, 0:256].bitcast(F32)[0:8, :]
    rcp = vop("tensor_copy", RRS[:], PB[5][0:8, 0:128], deps=(rtr,) + xn_last)
    bankg[5] = rcp
    rb_mm = None
    for tcq in range(2):
        for k in range(4):
            tc = tcq * 4 + k
            rb_mm = mm(PB[6][:, (tcq * 4 + k - tcq * 4) * 128:
                             (tcq * 4 + k - tcq * 4 + 1) * 128]
                       if False else
                       PB[6][:, k * 128:(k + 1) * 128],
                       SEL8A[:, tc * 128:(tc + 1) * 128], RRS[:],
                       k == 0, k == 3, deps=(rcp, ld_sel8), bank=6)
        ratio_tok_i = act(RATIOB[:, tcq * 512:(tcq + 1) * 512], PB[6][:, :],
                          AF.Copy, deps=(rb_mm,))
        bankg[6] = ratio_tok_i
        if tcq == 0:
            ratio_tok = [ratio_tok_i]
        else:
            ratio_tok.append(ratio_tok_i)
    if "ratio" in tap_d:
        ydma(tap_d["ratio"], RATIOB[:], deps=tuple(ratio_tok))
    if "mwt" in tap_d:
        ydma(tap_d["mwt"], MWT[:], deps=tuple(mwt_tok))

    QDBALL = arbf(11520, 4096)
    qdb_tok = [None] * NH
    qe_g = _tup(slot_g[1])
    for h in range(NH):
        qb_mm = mm(PB[4][:, 0:256], ONESRB[:], QR16[0:1, h * 256:(h + 1) * 256],
                   True, True, deps=(qr_tok[h],), bank=4)
        qdb_tok[h] = act(QDBALL[:, h * 256:(h + 1) * 256], PB[4][:, 0:256],
                         AF.Copy, deps=(qb_mm,) + qe_g)
        qe_g = ()
        bankg[4] = qdb_tok[h]

    KD = [arbf(10240, 128), arbf(10368, 128)]
    STT = [arf32(10496, 128), arf32(10752, 128)]
    SEND = [arf32(11008, 128), arf32(11264, 128)]
    chain_arena_g = arena_g1

    send_tok = [None] * NH
    stb_store = {}
    st_chain = {"stt_g": [chain_arena_g, chain_arena_g],
                "send_g": [chain_arena_g, chain_arena_g],
                "kd_g": [chain_arena_g, chain_arena_g],
                "pbU": None}

    def emit_chain_step(h, j, cst):
        if j == 0:
            cst["ms0"] = vop("memset", STT[0], 0.0, deps=_tup(cst["stt_g"][0]))
            cst["cur"], cst["oth"] = 0, 1
            cst["cur_tok"] = cst["ms0"]
        cur, oth = cst["cur"], cst["oth"]
        stb_store[(h, j)] = ydma(stb_d[h, j], STT[cur], deps=(cst["cur_tok"],),
                                 sem="stbst")
        kd0 = act(KD[0], KSBH[:, h, 2 * j, :], AF.Copy,
                  deps=(ev_tok[(h // 4, 2 * j)], kd_all) + _tup(cst["kd_g"][0]),
                  scale=KDCOL[:, 2 * h:2 * h + 1])
        kd1 = act(KD[1], KSBH[:, h, 2 * j + 1, :], AF.Copy,
                  deps=(ev_tok[(h // 4, 2 * j + 1)],) + _tup(cst["kd_g"][1]),
                  scale=KDCOL[:, 2 * h + 1:2 * h + 2])
        m1 = mm(PB[7][:, 0:128], KD[0], VSB3[:, 2 * j, 128 * h:128 * h + 128],
                True, False,
                deps=(kd0, ev_tok[(4 + h // 4, 2 * j)], cst["pbU"]), bank=7)
        m2 = mm(PB[7][:, 0:128], KD[1], VSB3[:, 2 * j + 1, 128 * h:128 * h + 128],
                False, True, deps=(kd1, ev_tok[(4 + h // 4, 2 * j + 1)]))
        cst["kd_g"] = [m1, m2]
        up = vop("scalar_tensor_tensor", STT[oth], STT[cur], BDBC[:, h:h + 1],
                 PB[7][:, 0:128], ALU.mult, ALU.add,
                 deps=(m2, cst["cur_tok"], e_bd, stb_store[(h, j)],
                       stb_store.get((h, j - 1)) if j > 0 else
                       _tup(cst["stt_g"][1])[0] if _tup(cst["stt_g"][1]) else None))
        bankg[7] = up
        cst["pbU"] = up
        cst["cur"], cst["oth"] = oth, cur
        cst["cur_tok"] = up
        if j == NBLK - 1:
            sd = vop("tensor_scalar", SEND[h % 2], STT[cst["cur"]],
                     MASKSB[:, 0:1], None, ALU.mult,
                     deps=(up, ld_masks) + _tup(cst["send_g"][h % 2]))
            send_tok[h] = ydma(ccin_d[128 * h:128 * h + 128, :], SEND[h % 2],
                               deps=(sd,), sem="sdst")
            cst["send_g"][h % 2] = send_tok[h]
            cst["stt_g"] = [sd, stb_store[(h, 3)]]

    gate_tok = {}
    wsm_g = [None, None]
    wsm_n = [0]

    def wsm_load(src_ap, deps=()):
        i = wsm_n[0] % 2
        wsm_n[0] += 1
        t = pdma(WSM[i].rearrange("p (c j) -> p c j", c=CH),
                 src_ap.rearrange("c p j -> p c j"),
                 deps=_tup(wsm_g[i]) + tuple(deps), sem="wsm%d" % i)
        return i, t

    for i_ in range(CH):
        wi, wt = wsm_load(wg16_d[i_])
        pbase = 0 if i_ % 2 == 0 else 2
        lastmm = [None, None]
        for c in range(CH):
            for th in range(2):
                first = (c == 0)
                lastmm[th] = mm(PB[pbase + th][:, :],
                                WSM[wi][:, c * 128:(c + 1) * 128],
                                XTL[:, 4 * th:4 * th + 4, c, :],
                                first, c == CH - 1,
                                deps=(wt,), bank=pbase + th)
            if c % 4 == 3:
                emit_chain_step(i_, c // 4, st_chain)
        if i_ < 2:
            gdep = (xlo1_g, prev_mwb)
        elif i_ < 10:
            gdep = (lg_mm_last[1],)
        elif i_ < 14:
            gdep = (lg_mm_last[1], wr_lo)
        else:
            gdep = (rb_mm,)
        for th in range(2):
            gate_tok[(i_, th)] = act(GATE3[:, i_, th * 512:(th + 1) * 512],
                                     PB[pbase + th][:, :], AF.Sigmoid,
                                     deps=(lastmm[th],) + gdep)
            bankg[pbase + th] = gate_tok[(i_, th)]
        wsm_g[wi] = lastmm[1]
    kd_g = st_chain["kd_g"]
    gate_all = tuple(gate_tok[(oc, th)] for oc in range(CH) for th in range(2))
    if "gate" in tap_d:
        ydma(tap_d["gate"], GATE[:], deps=gate_all)

    cc_tok = R.o("gpsimd",
                 lambda e: e.collective_compute(
                     "AllGather", ALU.bypass,
                     replica_groups=[[0, 1], [2, 3], [4, 5], [6, 7]],
                     ins=[ccin_d.opt()], outs=[ccout_d.opt()]),
                 deps=tuple(send_tok), sem="cc", n=1)
    if "ccout" in tap_d:
        ydma(tap_d["ccout"], ccout_d[:], deps=(cc_tok,))
    if "stb" in tap_d:
        ydma(tap_d["stb"], stb_d[:],
             deps=tuple(stb_store[(h, j)] for h in range(NH) for j in range(NBLK)))


    QBF = [arbf(0, 1024), arbf(1024, 1024)]
    QDQ = [arbf(2048, 1024), arbf(3072, 1024)]
    KT = [arbf(4096, 1024), arbf(5120, 1024)]
    DDH = [arbf(6144, 512), arbf(6656, 512)]
    STBH = [arf32(7680, 512), arf32(8704, 512)]
    KVH = [arf32(9728, 128), arf32(9984, 128)]
    KVR = [arf32(10240, 128), arf32(10496, 128)]
    SF = [arbf(10752, 128), arbf(10880, 128)]
    QKP = [arbf(11008, 256), arbf(11264, 256)]

    p4g0 = (dd_st[NH - 1], send_tok[NH - 1],
            stb_store[(NH - 1, 3)], kd_g[0], kd_g[1],
            prev_mwb, prev_lcp, rcp, lg_mm_last[1], ratio_tok[1])

    qbf_g = [p4g0, p4g0]
    qdq_g = [p4g0, p4g0]
    kt_g = [p4g0, p4g0]
    ddh_g = [p4g0, p4g0]
    stbh_g = [p4g0, p4g0]
    kvh_g = [p4g0, p4g0]
    kvr_g = [p4g0, p4g0]
    sf_g = [p4g0, p4g0]
    qkp_g = [p4g0, p4g0]

    head_pre = {}

    def prefetch_head(h, extra=()):
        i = h % 2
        toks = {}
        toks["w"] = wsm_load(wq16_d[h], deps=extra)
        toks["kt"] = ydma(KT[i].rearrange("d (a t) -> d a t", a=TC),
                          KSB[:, h * 1024:(h + 1) * 1024]
                          .rearrange("t (a d) -> t a d", a=TC),
                          deps=k_done + _tup(kt_g[i]) + tuple(extra),
                          transpose=True, sem="kt%d" % i)
        toks["dd"] = ydma(DDH[i], dd_d[h], deps=(dd_all,) + _tup(ddh_g[i]),
                          sem="ddl%d" % i)

        toks["stb"] = ydma(STBH[i].rearrange("p (j e) -> p j e", j=4),
                           stb_d[h].rearrange("j p e -> p j e"),
                           deps=(stb_store[(NH - 1, 3)],) + _tup(stbh_g[i]),
                           sem="stbl%d" % i)
        toks["kvr"] = ydma(KVR[i], ccout_d[128 * h:128 * h + 128, :],
                           deps=(cc_tok,) + _tup(kvr_g[i]), sem="kvr%d" % i)
        head_pre[h] = toks
        return toks

    prefetch_head(0)
    prefetch_head(1)

    mix_tok = {}
    for h in range(NH):
        i = h % 2
        pre = head_pre[h]
        wi = pre["w"][0]
        kvm = vop("tensor_scalar", KVH[i], KVR[i], MASKSB[:, 1:2], None, ALU.mult,
                  deps=(pre["kvr"], ld_masks) + _tup(kvh_g[i]))
        kvr_g[i] = kvm
        pbase = 0 if h % 2 == 0 else 2
        qlast = [None, None]
        for c in range(CH):
            for th in range(2):
                first = (c == 0)
                deps = [pre["w"][1]]
                if first:
                    deps += list(_tup(qbf_g[i]))
                qlast[th] = mm(PB[pbase + th][:, :],
                               WSM[wi][:, c * 128:(c + 1) * 128],
                               XTL[:, 4 * th:4 * th + 4, c, :],
                               first, c == CH - 1, deps=deps, bank=pbase + th)
        qsil = []
        for th in range(2):
            qsil.append(act(QBF[i][:, th * 512:(th + 1) * 512], PB[pbase + th][:, :],
                            AF.Silu, deps=(qlast[th],) + _tup(qbf_g[i])))
            bankg[pbase + th] = qsil[th]
        wsm_g[wi] = qlast[1]
        qdq = vop("tensor_tensor",
                  QDQ[i].rearrange("p (b m) -> p b m", b=NBLK),
                  QBF[i].rearrange("p (b m) -> p b m", b=NBLK),
                  QDBALL[:, h * 256:(h + 1) * 256]
                  .rearrange("p (b m) -> p b m", b=1).broadcast_to([128, NBLK, 256]),
                  ALU.mult,
                  deps=(qsil[0], qsil[1], qdb_tok[h]) + _tup(qdq_g[i]))
        if h in (0, 1):
            sfx = str(h)
            if "qbf" + sfx in tap_d:
                ydma(tap_d["qbf" + sfx], QBF[i], deps=(qsil[0], qsil[1]))
            if "kt" + sfx in tap_d:
                ydma(tap_d["kt" + sfx], KT[i], deps=(pre["kt"],))
            if h == 1 and "ddh1" in tap_d:
                ydma(tap_d["ddh1"], DDH[i], deps=(pre["dd"],))
            if h == 1 and "stbh1" in tap_d:
                ydma(tap_d["stbh1"].rearrange("p j e -> p j e"),
                     STBH[i].rearrange("p (j e) -> p j e", j=4), deps=(pre["stb"],))
        o_evs = []
        for j in range(NBLK):
            jp = j % 2
            sf = vop("scalar_tensor_tensor", SF[jp], KVH[i],
                     BDPBC[:, 4 * h + j:4 * h + j + 1],
                     STBH[i][:, 128 * j:128 * j + 128], ALU.mult, ALU.add,
                     deps=(kvm, pre["stb"], e_bdp) + _tup(sf_g[jp]))
            qks = []
            for ncn in range(2):
                qks.append(mm(PB[4 + ncn][:, 0:256],
                              KT[i][:, 256 * j + 128 * ncn:256 * j + 128 * ncn + 128],
                              QBF[i][:, 256 * j:256 * j + 256], True, True,
                              deps=(pre["kt"], qsil[0] if j < 2 else qsil[1]),
                              bank=4 + ncn))
            qkps = []
            for ncn in range(2):
                qkps.append(vop("tensor_tensor", QKP[ncn], PB[4 + ncn][:, 0:256],
                                DDH[i][:, 256 * ncn:256 * ncn + 256], ALU.mult,
                                deps=(qks[ncn], pre["dd"]) + _tup(qkp_g[ncn])))
                bankg[4 + ncn] = qkps[ncn]
            ob_ = 6 + jp
            mm(PB[ob_][:, 0:256], VSB3[:, 2 * j, 128 * h:128 * h + 128],
               QKP[0], True, False,
               deps=(qkps[0], ev_tok[(4 + h // 4, 2 * j)]), bank=ob_)
            mm(PB[ob_][:, 0:256], VSB3[:, 2 * j + 1, 128 * h:128 * h + 128],
               QKP[1], False, False, deps=(qkps[1], ev_tok[(4 + h // 4, 2 * j + 1)]))
            mo3 = mm(PB[ob_][:, 0:256], SF[jp], QDQ[i][:, 256 * j:256 * j + 256],
                     False, True, deps=(sf, qdq))
            sf_g[jp] = mo3
            qkp_g = [mo3, mo3]
            if h == 1 and j == 0 and "sf1" in tap_d:
                ydma(tap_d["sf1"], SF[jp], deps=(sf,))
            if h == 1 and j == 0 and "qkp1" in tap_d:
                ydma(tap_d["qkp1"][:, 0:256], QKP[0], deps=(qkps[0],))
                ydma(tap_d["qkp1"][:, 256:512], QKP[1], deps=(qkps[1],))
            mixg = ()
            if h < 4:
                mixg = xn_last
            elif h < 14:
                mixg = (dd_all, qdb_tok[NH - 1])
            ev = act(MIX3[:, h, 256 * j:256 * j + 256], PB[ob_][:, 0:256], AF.Copy,
                     deps=(mo3,) + mixg)
            bankg[ob_] = ev
            o_evs.append(ev)
        kt_g[i] = qks[1]
        ddh_g[i] = qkps[1]
        stbh_g[i] = sf
        kvh_g[i] = sf
        qbf_g[i] = (qks[1], qdq)
        qdq_g[i] = mo3
        if h + 2 < NH:
            prefetch_head(h + 2)
        mix_tok[h] = tuple(o_evs)
    mix_all = tuple(t for h in range(NH) for t in mix_tok[h])
    p4_done = (mix_tok[NH - 2][-1], mix_tok[NH - 1][-1],
               qdq_g[0], qdq_g[1], sf_g[0], sf_g[1])
    if "mix" in tap_d:
        ydma(tap_d["mix"], MIX[:], deps=mix_all)

    SQ = [arbf(0, 1024), arbf(1024, 1024)]
    EBB = arbf(2048, 2048)[0:64, :]
    GA = arf32(4096, 1024)
    GB = arf32(6144, 1024)
    VROW = arf32(8192, 1024)[0:1, :]
    RROW = arf32(10240, 1024)[0:1, :]

    ld_eb = pdma(EBB, eb_d[:], deps=p4_done, sem="ebl")

    sq_all = [None] * NH
    var_mm = None
    for h in range(NH):
        i = h % 2
        prev_sq = sq_all[h - 2] if h >= 2 else None
        sq_all[h] = act(SQ[i], MIX3[:, h, :], AF.Square,
                        deps=mix_tok[h] + p4_done + ((var_mm,) if h >= 2 else ()))
        for th in range(2):
            var_mm = mm(PB[th][0:1, :], INVD[:],
                        SQ[i][:, th * 512:(th + 1) * 512],
                        h == 0, h == NH - 1,
                        deps=(sq_all[h], ld_invd), bank=th)
    vcp = []
    for th in range(2):
        vcp.append(vop("tensor_copy", VROW[:, th * 512:(th + 1) * 512],
                       PB[th][0:1, :], deps=(var_mm,) + p4_done))
    bankg[0] = vcp[0]
    bankg[1] = vcp[1]
    veps = vop("tensor_scalar", RROW, VROW, RMS_EPS, None, ALU.add,
               deps=tuple(vcp) + p4_done)
    vln = act(VROW, RROW, AF.Ln, deps=(veps,))
    vlnm = vop("tensor_scalar", RROW, VROW, -0.5, None, ALU.mult, deps=(vln,))
    vrs = act(VROW, RROW, AF.Exp, deps=(vlnm,))
    rc_mm = None
    for tc in range(TC):
        rc_mm = mm(PB[2][:, tc:tc + 1], VROW[0:1, tc * 128:(tc + 1) * 128], ONE1[:],
                   tc == 0, tc == TC - 1, deps=(vrs, ld_one1), bank=2)
    rstd_tok = vop("tensor_copy", RSTDC[:], PB[2][:, 0:8], deps=(rc_mm,))
    bankg[2] = rstd_tok
    if "rstd" in tap_d:
        ydma(tap_d["rstd"], RSTDC[:], deps=(rstd_tok,))

    g_tok = [None] * CH
    ga_g = p4_done
    gb_g = p4_done
    b_last = None
    for fc in range(CH):
        pbase = 3 if fc % 2 == 0 else 5
        bmmt = [None, None]
        for th in range(2):
            bmmt[th] = mm(PB[pbase + th][:, :], EBB[:, fc * 128:(fc + 1) * 128],
                          MWT[:, th * 512:(th + 1) * 512], True, True,
                          deps=(ld_eb, mwt_tok[-1]), bank=pbase + th)
            b_last = bmmt[th]
        ga = vop("tensor_tensor", GA.rearrange("p (a t) -> p a t", a=TC),
                 XTL[:, :, fc, :],
                 RATIOB.rearrange("p (a t) -> p a t", a=TC), ALU.mult,
                 deps=(ratio_tok[0], ratio_tok[1]) + _tup(ga_g))
        gbs = []
        for th in range(2):
            gbs.append(vop("tensor_tensor", GB[:, th * 512:(th + 1) * 512],
                           GA[:, th * 512:(th + 1) * 512], PB[pbase + th][:, :],
                           ALU.add, deps=(ga, bmmt[th]) + _tup(gb_g)))
            bankg[pbase + th] = gbs[th]
        g_tok[fc] = act(VSBL[:, :, fc, :], GB, AF.Silu,
                        deps=tuple(gbs) + (mix_all[-1], p4_done[-1], p4_done[-2]))
        ga_g = gbs[1]
        gb_g = g_tok[fc]
    g_all = tuple(g_tok)

    gt_tok = [None] * TC
    for tc in range(TC):
        gt_tok[tc] = ydma(KSB[:, tc * 2048:(tc + 1) * 2048]
                          .rearrange("p (c f) -> p c f", c=CH),
                          VSB[:, tc * 2048:(tc + 1) * 2048]
                          .rearrange("p (c t) -> p c t", c=CH),
                          deps=g_all + (head_pre[NH - 1]["kt"], kd_g[1]),
                          transpose=True, sem="gtt")
    if "gt" in tap_d:
        ydma(tap_d["gt"], KSB[:], deps=tuple(gt_tok))

    ZT = arbf(12288, 1024)
    z_tok = [None] * CH
    zt_g = ()
    for oc in range(CH):
        zt = vop("tensor_tensor", ZT, MIX3[:, oc, :], GATE3[:, oc, :], ALU.mult,
                 deps=(sq_all[oc], gate_tok[(oc, 0)], gate_tok[(oc, 1)]) +
                 mix_tok[oc] + _tup(zt_g))
        z_tok[oc] = vop("tensor_scalar", MIX3[:, oc, :], ZT, NWC[:, oc:oc + 1],
                        None, ALU.mult, deps=(zt, ld_nw, sq_all[oc]))
        zt_g = z_tok[oc]

    if "z" in tap_d:
        ydma(tap_d["z"], MIX[:], deps=tuple(z_tok))

    WO = [arbf(0, 2048), arbf(2048, 2048)]
    WOF = arf32(4096, 2048)
    XH = [arf32(8192, 1024), arf32(8192, 1024)]
    YV = [arf32(10240, 1024), arf32(12288, 1024)]
    PO = [arf32(14336, 1024), arf32(14336, 1024)]

    wo_g = [(var_mm, sq_all[15], ld_eb, b_last),
            (var_mm, sq_all[15], ld_eb, b_last)]
    wof_g = (var_mm, sq_all[15]) + _tup(gb_g) + _tup(ga_g)
    xh_g = [(rstd_tok,), (rstd_tok,)]
    yv_g = [(vrs, rstd_tok, g_tok[15]), (vrs, rstd_tok, g_tok[15], zt_g)]
    po_g = [(vrs, rstd_tok, zt_g), (vrs, rstd_tok, zt_g)]
    out_st = []
    for fh in range(2):
        for tq in range(2):
            lastmm = {}
            wts = {}
            for op in range(8):
                wi = op % 2
                if wi == 0:
                    wts[op] = pdma(WO[0], wo8_d[fh, op], deps=_tup(wo_g[0]),
                                   sem="wol0")
                else:
                    wf = ydma(WOF, wo8_d[fh, op],
                              deps=_tup(wo_g[1]) + _tup(wof_g), sem="xh0")
                    wts[op] = vop("tensor_copy", WO[1], WOF, deps=(wf,))
                    wof_g = wts[op]
                for oi in range(2):
                    oc = 2 * op + oi
                    for ti in range(4):
                        tc = 4 * tq + ti
                        for q in range(2):
                            bank = ti * 2 + q
                            first = (oc == 0)
                            lastmm[(ti, q)] = mm(
                                PB[bank][:, :],
                                MIX3[:, oc, tc * 128:(tc + 1) * 128],
                                WO[wi][:, oi * 1024 + q * 512:oi * 1024 + (q + 1) * 512],
                                first, oc == CH - 1,
                                deps=(wts[op], z_tok[oc]), bank=bank)
                wo_g[wi] = lastmm[(3, 1)]
            for ti in range(4):
                tc = 4 * tq + ti
                ip = ti % 2
                xh = ydma(XH[ip], xs_d[tc * 128:(tc + 1) * 128,
                                       fh * 1024:(fh + 1) * 1024],
                          deps=_tup(xh_g[0]) + _tup(xh_g[1]), sem="xh1")
                yv = []
                for q in range(2):
                    yv.append(act(YV[ip][:, q * 512:(q + 1) * 512],
                                  PB[ti * 2 + q][:, :], AF.Copy,
                                  deps=(lastmm[(ti, q)], rstd_tok) + _tup(yv_g[ip]),
                                  scale=RSTDC[:, tc:tc + 1]))
                    bankg[ti * 2 + q] = yv[q]
                if fh == 0 and tq == 0 and ti == 0 and "yv00" in tap_d:
                    ydma(tap_d["yv00"], YV[ip], deps=tuple(yv))
                pm = vop("tensor_tensor", PO[ip], YV[ip],
                         KSB[:, tc * 2048 + fh * 1024:tc * 2048 + fh * 1024 + 1024],
                         ALU.mult,
                         deps=tuple(yv) + (gt_tok[TC - 1],) + _tup(po_g[ip]))
                po = vop("tensor_tensor", YV[ip], PO[ip], XH[ip], ALU.add,
                         deps=(pm, xh))
                po_g[ip] = po
                st = ydma(out_d[tc * 128:(tc + 1) * 128, fh * 1024:(fh + 1) * 1024],
                          YV[ip], deps=(po,), sem="ost%d" % ip)
                out_st.append(st)
                yv_g[ip] = (po, st)
                xh_g[ip] = (po,)

    final_counts = {k: R.counts[k] for k in R.counts if R.counts[k] > 0}

    for engname, ops in R.stream.items():
        for fn, deps, sem, n in ops:
            for dsem, dval in deps:
                assert dval <= R.counts[dsem], (engname, dsem, dval, R.counts[dsem])

    def make_body(engname):
        def body(eng):
            last = {}
            for fn, deps, sem, n in R.stream[engname]:
                for dsem, dval in deps:
                    if last.get(dsem, 0) < dval:
                        eng.wait_ge(sems[dsem], dval)
                        last[dsem] = dval
                inst = fn(eng)
                inst.then_inc(sems[sem], n)
            if engname == "sync":
                for kk in ("ost0", "ost1", "y"):
                    if final_counts.get(kk):
                        eng.wait_ge(sems[kk], final_counts[kk])
        return body

    with nc.Block() as block:
        block.sync(make_body("sync"))
        block.gpsimd(make_body("gpsimd"))
        block.scalar(make_body("scalar"))
        block.vector(make_body("vector"))
        block.tensor(make_body("tensor"))

    for g in reversed(psg):
        g.__exit__(None, None, None)
    for g in reversed(sbg):
        g.__exit__(None, None, None)
    return nc


def _shared_arrays(wqkv, wgate, wout, wrout, rbias, ebias, normw, slope):
    f32 = np.float32
    rows_k = (384 * np.arange(NH)[:, None] + 128 + np.arange(HD)[None, :]).reshape(-1)
    rows_v = (384 * np.arange(NH)[:, None] + 256 + np.arange(HD)[None, :]).reshape(-1)
    rows_q = (384 * np.arange(NH)[:, None] + np.arange(HD)[None, :]).reshape(-1)
    wsel = wqkv[np.concatenate([rows_k, rows_v])]
    wkv8 = np.ascontiguousarray(
        wsel.reshape(8, 512, 16, 128).transpose(0, 2, 3, 1))
    wq16 = np.ascontiguousarray(
        wqkv[rows_q].reshape(16, 128, 16, 128).transpose(0, 2, 3, 1))
    wg16 = np.ascontiguousarray(
        wgate.reshape(16, 128, 16, 128).transpose(0, 2, 3, 1))
    woT = np.ascontiguousarray(wout.T)
    wo8 = np.ascontiguousarray(
        woT.reshape(8, 2, 128, 2, 1024).transpose(3, 0, 2, 1, 4)
        .reshape(2, 8, 128, 2048))
    wr16 = np.ascontiguousarray(wrout.T.reshape(16, 128, 64))
    rb = np.ascontiguousarray(rbias.reshape(64, 1)).astype(f32)
    nw = np.ascontiguousarray(normw.reshape(16, 128).T).astype(f32)
    slopes = np.ascontiguousarray(slope.reshape(1, 16)).astype(f32)

    m = np.arange(1, 257, dtype=f32)
    identb = np.eye(128, dtype=NPBF)
    identf = np.eye(128, dtype=f32)
    iotam = m.reshape(1, 256).copy()
    jneg = np.tile(np.arange(4, dtype=f32) * (-256.0), 16).reshape(1, 64)
    iotanm = np.stack([np.arange(1, 129, dtype=f32) - 256.0,
                       np.arange(129, 257, dtype=f32) - 256.0], axis=1)
    sel8 = np.zeros((8, 1024), dtype=f32)
    for k in range(8):
        sel8[k, k * 128:(k + 1) * 128] = 1.0
    invd = np.full((128, 1), 1.0 / H, dtype=NPBF)
    p = np.arange(128, dtype=f32)
    idxt = np.zeros((128, 512), dtype=f32)
    for jc in range(2):
        idxt[:, jc * 256:(jc + 1) * 256] = (m[None, :] - 1.0) - (128 * jc + p[:, None])
    maskt = (idxt >= 0).astype(f32)
    onesr = np.ones((1, 128), dtype=f32)
    one1 = np.ones((1, 1), dtype=f32)
    return dict(wkv8=wkv8.astype(f32), wq16=wq16.astype(f32), wg16=wg16.astype(f32),
                wo8=wo8.astype(f32), wr16=wr16.astype(f32), rb=rb,
                eb=np.ascontiguousarray(ebias).astype(f32), nw=nw, slopes=slopes,
                identb=identb, identf=identf, iotam=iotam, jneg=jneg.astype(f32),
                iotanm=iotanm.astype(f32), sel8=sel8, invd=invd,
                idxt=idxt, maskt=maskt, onesr=onesr, one1=one1,
                onesrb=np.ones((1, 128), dtype=NPBF))


def make_in_maps(inputs):
    x = np.asarray(inputs["L0_attn_hidden_states"], dtype=np.float32)
    shared = _shared_arrays(
        np.asarray(inputs["L0_attn_w_qkv"], np.float32),
        np.asarray(inputs["L0_attn_w_gate"], np.float32),
        np.asarray(inputs["L0_attn_w_out"], np.float32),
        np.asarray(inputs["L0_ffn_router_weight"], np.float32),
        np.asarray(inputs["L0_ffn_router_bias"], np.float32),
        np.asarray(inputs["L0_ffn_expert_bias"], np.float32),
        np.asarray(inputs["L0_attn_norm_w"], np.float32),
        np.asarray(inputs["L0_attn_slope_rate"], np.float32),
    )
    in_maps = []
    for core in range(8):
        b, half = core >> 1, core & 1
        mdict = dict(shared)
        mdict["xs"] = np.ascontiguousarray(x[b, half * T:(half + 1) * T, :])
        send = 1.0 if half == 0 else 0.0
        recv = 0.0 if half == 0 else 1.0
        mdict["masks"] = np.tile(np.array([[send, recv]], np.float32), (128, 1))
        in_maps.append(mdict)
    return in_maps


def kernel(**inputs):
    in_maps = make_in_maps(inputs)
    nc = build()
    res = run_bass_kernel_spmd(nc, in_maps, core_ids=list(range(8)))
    out = np.empty((4, 2048, H), dtype=np.float32)
    for core in range(8):
        b, half = core >> 1, core & 1
        out[b, half * T:(half + 1) * T, :] = res.results[core]["out"]
    return out


# version 08
# speedup vs baseline: 1.1651x; 1.0238x over previous
import numpy as np
import ml_dtypes
import concourse.bass as bass
import concourse.mybir as mybir
from concourse.bass_utils import run_bass_kernel_spmd

F32 = mybir.dt.float32
BF16 = mybir.dt.bfloat16
AF = mybir.ActivationFunctionType
ALU = mybir.AluOpType
AX = mybir.AxisListType
NPBF = ml_dtypes.bfloat16

H = 2048
NH = 16
HD = 128
T = 1024
TC = 8
CH = 16
NBLK = 4
NE = 64
LN_EPS = 1e-5
RMS_EPS = 1e-6

ENG2SEM = {"sync": "y", "gpsimd": "p", "scalar": "s", "vector": "v", "tensor": "t"}



def _tup(x):
    if x is None:
        return ()
    if isinstance(x, tuple) and len(x) == 2 and isinstance(x[0], str) \
            and isinstance(x[1], int):
        return (x,)
    return tuple(t for t in x if t is not None)

class Rec:
    def __init__(self):
        self.stream = {e: [] for e in ENG2SEM}
        self.counts = {k: 0 for k in ("y", "p", "s", "v", "t", "cc")}

    def o(self, eng, fn, deps=(), sem=None, n=1):
        if sem is None:
            sem = ENG2SEM[eng]
        self.counts[sem] = self.counts.get(sem, 0) + n
        self.stream[eng].append((fn, tuple(d for d in deps if d is not None), sem, n))
        return (sem, self.counts[sem])


def build(taps=()):
    nc = bass.Bass("TRN2", target_bir_lowering=False, dynamic_dma_scratch_size=16384)

    def din(name, shape, dt=F32):
        return nc.dram_tensor(name, shape, dt, kind="ExternalInput").ap()

    xs_d = din("xs", [T, H])
    wkv8_d = din("wkv8", [8, 16, 128, 512])
    wq16_d = din("wq16", [16, 16, 128, 128])
    wg16_d = din("wg16", [16, 16, 128, 128])
    wo8_d = din("wo8", [2, 8, 128, 2048])
    wr16_d = din("wr16", [16, 128, 64])
    rb_d = din("rb", [64, 1])
    eb_d = din("eb", [64, H])
    nw_d = din("nw", [128, 16])
    slopes_d = din("slopes", [1, 16])
    masks_d = din("masks", [128, 2])
    identb_d = din("identb", [128, 128], BF16)
    identf_d = din("identf", [128, 128])
    iotam_d = din("iotam", [1, 256])
    jneg_d = din("jneg", [1, 64])
    iotanm_d = din("iotanm", [128, 2])
    sel8_d = din("sel8", [8, 1024])
    invd_d = din("invd", [128, 1], BF16)
    idxt_d = din("idxt", [128, 512])
    maskt_d = din("maskt", [128, 512])
    onesr_d = din("onesr", [1, 128])
    onesrb_d = din("onesrb", [1, 128], BF16)
    one1_d = din("one1", [1, 1])

    out_d = nc.dram_tensor("out", [T, H], F32, kind="ExternalOutput").ap()
    tap_d = {}
    for tp, shape, dt in (
        ("xt", [128, CH * T], BF16), ("ksb", [128, TC * H], BF16),
        ("vsb", [128, TC * H], BF16), ("mix", [128, NH * T], BF16),
        ("gate", [128, CH * T], BF16), ("lg", [64, T], F32),
        ("gt", [128, TC * H], BF16), ("ratio", [128, T], BF16),
        ("mwt", [64, T], BF16), ("rstd", [128, 8], F32),
        ("stb", [NH, NBLK, 128, 128], F32), ("ccout", [2 * NH * HD, 128], F32),
        ("qbf0", [128, T], BF16), ("qbf1", [128, T], BF16),
        ("kt0", [128, T], BF16), ("kt1", [128, T], BF16),
        ("ddh1", [128, 512], BF16), ("qdh1", [128, 256], BF16),
        ("sf1", [128, 128], BF16), ("qkp1", [128, 512], BF16),
        ("stbh1", [128, 4, 128], F32),
        ("z", [128, NH * T], BF16), ("yv00", [128, 1024], F32),
    ):
        if tp in taps:
            tap_d[tp] = nc.dram_tensor("tap_" + tp, shape, dt, kind="ExternalOutput").ap()

    xlo_d = nc.dram_tensor("xlo_i", [T, H], BF16, kind="Internal").ap()
    dd_d = nc.dram_tensor("dd_i", [NH, 128, 512], BF16, kind="Internal").ap()
    qd_d = nc.dram_tensor("qd_i", [NH, 128, 256], BF16, kind="Internal").ap()
    stb_d = nc.dram_tensor("stb_i", [NH, NBLK, 128, 128], F32, kind="Internal").ap()
    ccin_d = nc.dram_tensor("ccin_i", [NH * HD, 128], F32, kind="Internal").ap()
    ccout_d = nc.dram_tensor("ccout_i", [2 * NH * HD, 128], F32, kind="Internal").ap()

    sbg = []

    def sb(name, shape, dt, **kw):
        g = nc.sbuf_tensor(name, shape, dt, **kw)
        t = g.__enter__()
        sbg.append(g)
        return t

    XT = sb("XT", [128, CH * T], BF16, is_transpose_dest=True)
    KSB = sb("KSB", [128, TC * H], BF16, is_transpose_dest=True)
    VSB = sb("VSB", [128, TC * H], BF16)
    MIX = sb("MIX", [128, NH * T], BF16)
    GATE = sb("GATE", [128, CH * T], BF16)
    WSM = [sb("WSM0", [128, 2048], BF16), sb("WSM1", [128, 2048], BF16)]
    AR = sb("AR", [128, 16384], BF16, is_transpose_dest=True)
    IDB = sb("IDB", [128, 128], BF16)
    IDF = sb("IDF", [128, 128], F32)
    ONESR = sb("ONESR", [1, 128], F32)
    ONESRB = sb("ONESRB", [1, 128], BF16)
    ONE1 = sb("ONE1", [1, 1], F32)
    JNEG = sb("JNEG", [1, 64], F32)
    IOTANM = sb("IOTANM", [128, 2], F32)
    INVD = sb("INVD", [128, 1], BF16)
    SLOPES = sb("SLOPES", [1, 16], F32)
    SNEGR = sb("SNEGR", [1, 16], F32)
    SBC = sb("SBC", [128, 16], F32)
    SNEGBC = sb("SNEGBC", [128, 16], F32)
    BDBC = sb("BDBC", [128, 16], F32)
    BDPBC = sb("BDPBC", [128, 64], F32)
    KDCOL = sb("KDCOL", [128, 32], F32)
    NWC = sb("NWC", [128, 16], F32)
    MASKSB = sb("MASKSB", [128, 2], F32)
    RBC = sb("RBC", [64, 1], F32)
    MWT = sb("MWT", [64, T], BF16)
    RATIOB = sb("RATIOB", [128, T], BF16)
    RAT8 = sb("RAT8", [128, 8], F32)
    RSTDC = sb("RSTDC", [128, 8], F32)
    STATS = sb("STATS", [128, 24], F32)
    TOPC = sb("TOPC", [128, 24], F32)
    ROWS = sb("ROWSC", [1, 144], F32)

    psg = [nc.psum_tensor("pb%d" % i, [128, 512], F32) for i in range(8)]
    PB = [g.__enter__() for g in psg]

    semnames = ["y", "p", "s", "v", "t", "cc", "cn", "xs0", "xs1", "xt0", "xt1",
                "xlo0", "xlo1", "l12", "wr", "sel", "ixm",
                "wkv0", "wkv1", "wkv2", "wsm0", "wsm1",
                "ddst", "qdst", "stbst", "sdst",
                "kt0", "kt1", "ddl0", "ddl1", "qdl0", "qdl1",
                "stbl0", "stbl1", "kvr0", "kvr1", "ebl",
                "gtt", "xh0", "xh1", "wol0", "wol1", "ost0", "ost1"]
    sems = {k: nc.alloc_semaphore("sem_" + k) for k in semnames}
    R = Rec()

    def arf32(lo, n):
        return AR[:, lo:lo + 2 * n].bitcast(F32)

    def arbf(lo, n):
        return AR[:, lo:lo + n]

    XTL = XT.rearrange("p (a c t) -> p a c t", a=TC, c=CH)
    KSBH = KSB.rearrange("p (h a d) -> p h a d", h=NH, a=TC)
    VSBL = VSB.rearrange("p (a c f) -> p a c f", a=TC, c=CH)
    VSB3 = VSB.rearrange("p (a o) -> p a o", a=TC)
    MIX3 = MIX.rearrange("p (h t) -> p h t", h=NH)
    GATE3 = GATE.rearrange("p (c t) -> p c t", c=CH)

    def ydma(out, in_, deps=(), transpose=False, sem=None):
        if transpose:
            return R.o("sync", lambda e, o=out, i=in_: e.dma_start(out=o, in_=i, transpose=True), deps, n=16, sem=sem)
        return R.o("sync", lambda e, o=out, i=in_: e.dma_start(out=o, in_=i), deps, n=16, sem=sem)

    def pdma(out, in_, deps=(), sem=None):
        return R.o("gpsimd", lambda e, o=out, i=in_: e.dma_start(out=o, in_=i), deps, n=16, sem=sem)

    bankg = [None] * 8

    def mm(bank_or_out, lhsT, rhs, start, stop, deps=(), bank=None):
        out = bank_or_out
        if bank is not None and start:
            deps = tuple(deps) + (bankg[bank],)
        return R.o("tensor", lambda e, o=out, l=lhsT, r=rhs, st=start, sp=stop:
                   e.matmul(o, l, r, start=st, stop=sp), deps)

    def tr(out, in_, ident, deps=(), bank=None):
        if bank is not None:
            deps = tuple(deps) + (bankg[bank],)
        return R.o("tensor", lambda e, o=out, i=in_, d=ident: e.transpose(o, i, d), deps)

    def act(out, in_, func, deps=(), bias=0.0, scale=1.0, accum_out=None):
        if func is AF.Copy and not (isinstance(bias, float) and isinstance(scale, float)):
            func = AF.Identity
        return R.o("scalar", lambda e, o=out, i=in_, f=func, b=bias, sc=scale, a=accum_out:
                   e.activation(o, i, f, bias=b, scale=sc, accum_out=a), deps)

    def vop(name, *args, deps=()):
        return R.o("vector", lambda e, nm=name, aa=args: getattr(e, nm)(*aa), deps)

    for dst, srcd in ((IDB[:], identb_d[:]), (IDF[:], identf_d[:]),
                      (ONESR[:], onesr_d[:]), (ONE1[:], one1_d[:]),
                      (JNEG[:], jneg_d[:]),
                      (IOTANM[:], iotanm_d[:]), (INVD[:], invd_d[:]),
                      (SLOPES[:], slopes_d[:]), (NWC[:], nw_d[:]),
                      (ONESRB[:], onesrb_d[:]),
                      (MASKSB[:], masks_d[:]), (RBC[:], rb_d[:])):
        CT = ydma(dst, srcd, sem="cn")
    ld_idb = ld_idf = ld_onesr = ld_one1 = ld_iotam = ld_jneg = CT
    ld_iotanm = ld_invd = ld_slopes = ld_nw = ld_masks = ld_rb = CT

    XSL = [arf32(4096 * i, 2048) for i in range(4)]
    XNB = [MIX[:, 0:2048], MIX[:, 2048:4096]]
    SQD = GATE[:, 2048:4096]
    XLO1 = GATE[:, 0:2048]

    xt_tok = [None] * TC
    xlo_st = [None] * TC
    ops = {tc: {} for tc in range(TC)}

    def p0_g1(tc):
        i = tc % 4
        c0 = 8 * (tc % 3)
        o = ops[tc]
        pv = ops.get(tc - 4, {})
        o["xld"] = pdma(XSL[i], xs_d[tc * 128:(tc + 1) * 128, :],
                        deps=(pv.get("x1"), pv.get("xlo")), sem="xs%d" % (i % 2))
        o["sum"] = vop("reduce_sum", STATS[:, c0:c0 + 1], XSL[i], AX.X,
                       deps=(o["xld"],))
        sqg = ops.get(tc - 1, {}).get("sq2")
        o["sq1"] = act(SQD[:, 0:1024], XSL[i][:, 0:1024], AF.Square,
                       deps=(o["xld"], sqg),
                       accum_out=STATS[:, c0 + 1:c0 + 2])
        o["sq2"] = act(SQD[:, 1024:2048], XSL[i][:, 1024:2048], AF.Square,
                       deps=(o["xld"],),
                       accum_out=STATS[:, c0 + 2:c0 + 3])

    def p0_g2(tc):
        i = tc % 4
        j = tc % 2
        c0 = 8 * (tc % 3)
        o = ops[tc]
        o["mean"] = vop("tensor_scalar", STATS[:, c0 + 3:c0 + 4],
                        STATS[:, c0:c0 + 1], 1.0 / H, None, ALU.mult,
                        deps=(o["sum"],))
        o["ex2"] = vop("tensor_scalar", STATS[:, c0 + 4:c0 + 5],
                       STATS[:, c0 + 1:c0 + 2], STATS[:, c0 + 2:c0 + 3], 1.0 / H,
                       ALU.add, ALU.mult, deps=(o["sq1"], o["sq2"]))
        o["m2"] = vop("tensor_tensor", STATS[:, c0 + 5:c0 + 6],
                      STATS[:, c0 + 3:c0 + 4], STATS[:, c0 + 3:c0 + 4],
                      ALU.mult, deps=(o["mean"],))
        o["var"] = vop("tensor_scalar", STATS[:, c0 + 6:c0 + 7],
                       STATS[:, c0 + 4:c0 + 5], STATS[:, c0 + 5:c0 + 6], LN_EPS,
                       ALU.subtract, ALU.add, deps=(o["ex2"], o["m2"]))
        o["ln"] = act(STATS[:, c0 + 7:c0 + 8], STATS[:, c0 + 6:c0 + 7],
                      AF.Ln, deps=(o["var"],))
        o["lnm"] = vop("tensor_scalar", STATS[:, c0 + 6:c0 + 7],
                       STATS[:, c0 + 7:c0 + 8], -0.5, None, ALU.mult,
                       deps=(o["ln"],))
        o["rstd"] = act(STATS[:, c0 + 7:c0 + 8], STATS[:, c0 + 6:c0 + 7],
                        AF.Exp, deps=(o["lnm"],))
        o["nm"] = vop("scalar_tensor_tensor", STATS[:, c0 + 5:c0 + 6],
                      STATS[:, c0 + 3:c0 + 4], -1.0,
                      STATS[:, c0 + 7:c0 + 8], ALU.mult, ALU.mult,
                      deps=(o["rstd"], o["m2"]))
        o["xn"] = act(XNB[j], XSL[i], AF.Copy,
                      deps=(o["rstd"], o["nm"], o["sq1"], o["sq2"],
                            ops.get(tc - 2, {}).get("xt"),
                            ops.get(tc - 2, {}).get("x1")),
                      bias=STATS[:, c0 + 5:c0 + 6], scale=STATS[:, c0 + 7:c0 + 8])
        o["xt"] = ydma(XTL[:, tc], XNB[j].rearrange("t (c p) -> t c p", c=CH),
                       deps=(o["xn"],), transpose=True, sem="xt%d" % j)
        xt_tok[tc] = o["xt"]

    def p0_g3(tc):
        i = tc % 4
        j = tc % 2
        c0 = 8 * (tc % 3)
        o = ops[tc]
        o["x1"] = vop("scalar_tensor_tensor", XLO1, XSL[i],
                      STATS[:, c0 + 7:c0 + 8], XNB[j], ALU.mult, ALU.subtract,
                      deps=(o["xn"], ops.get(tc - 1, {}).get("x2")))
        XLO2i = arbf(4096 * i, 2048)
        o["x2"] = act(XLO2i, XLO1, AF.Copy, deps=(o["x1"],),
                      bias=STATS[:, c0 + 5:c0 + 6])
        o["xlo"] = ydma(xlo_d[tc * 128:(tc + 1) * 128, :], XLO2i,
                        deps=(o["x2"],), sem="xlo%d" % (i % 2))
        xlo_st[tc] = o["xlo"]

    for k in range(TC + 4):
        if k - 4 >= 0:
            p0_g3(k - 4)
        if 0 <= k - 2 < TC:
            p0_g2(k - 2)
        if k < TC:
            p0_g1(k)
    xt_all = tuple(xt_tok)
    xsl_g = [(ops[6]["x1"], ops[6]["xlo"]), (ops[7]["x1"], ops[7]["xlo"])]
    xnb_g = [(ops[6]["xt"], ops[6]["x1"]), (ops[7]["xt"], ops[7]["x1"])]
    xlo1_g = ops[7]["x2"]
    sq_last = ops[7]["sq2"]
    xn_last = (ops[6]["xt"], ops[7]["xt"], ops[6]["x1"], ops[7]["x1"])
    if "xt" in tap_d:
        ydma(tap_d["xt"], XT[:], deps=xt_all)

    sneg_tok = vop("tensor_scalar", SNEGR[:], SLOPES[:], -1.0, None, ALU.mult,
                   deps=(ld_slopes,))
    bdr_u = vop("tensor_scalar", ROWS[:, 80:96], SLOPES[:], -256.0, None,
                ALU.mult, deps=(ld_slopes,))
    bdrow = act(ROWS[:, 0:16], ROWS[:, 80:96], AF.Exp, deps=(bdr_u,))
    bdp_u = vop("tensor_tensor",
                ROWS[:, 16:80].rearrange("a (e j) -> a e j", e=16),
                JNEG[:].rearrange("a (e j) -> a e j", e=16),
                SLOPES[0:1, :].rearrange("a (e o) -> a e o", o=1).broadcast_to([1, 16, 4]),
                ALU.mult, deps=(ld_jneg, ld_slopes))
    bdprow = act(ROWS[:, 80:144], ROWS[:, 16:80], AF.Exp, deps=(bdp_u, bdrow))
    b1 = mm(PB[4][:, 0:16], ONESR[:], SLOPES[:], True, False,
            deps=(ld_onesr, ld_slopes), bank=4)
    b2 = mm(PB[4][:, 16:32], ONESR[:], ROWS[:, 0:16], False, False, deps=(bdrow,))
    b3 = mm(PB[4][:, 32:96], ONESR[:], ROWS[:, 80:144], False, True, deps=(bdprow,))
    e_sbc = vop("tensor_copy", SBC[:], PB[4][:, 0:16], deps=(b3,))
    e_bd = vop("tensor_copy", BDBC[:], PB[4][:, 16:32], deps=(b3,))
    e_bdp = vop("tensor_copy", BDPBC[:], PB[4][:, 32:96], deps=(b3,))
    bankg[4] = e_bdp
    e_sneg = vop("tensor_scalar", SNEGBC[:], SBC[:], -1.0, None, ALU.mult,
                 deps=(e_sbc,))
    kd_prev = None
    for h in range(NH):
        cs = 16 + (h % 2) * 2
        u = vop("tensor_scalar", STATS[:, cs:cs + 2], IOTANM[:],
                SBC[:, h:h + 1], None, ALU.mult,
                deps=(ld_iotanm, e_sbc, kd_prev))
        kd_prev = act(KDCOL[:, 2 * h:2 * h + 2], STATS[:, cs:cs + 2],
                      AF.Exp, deps=(u,))
    kd_all = kd_prev

    DDA = MIX[:, 4096:5120].bitcast(F32)
    DDB = MIX[:, 5120:6144].bitcast(F32)
    DDC = MIX[:, 6144:6656]
    IDXA = MIX[:, 6656:7680].bitcast(F32)
    MKA = MIX[:, 7680:8704].bitcast(F32)
    IOTA_A = MIX[:, 8704:9216].bitcast(F32)[0:1, :]
    QR16 = MIX[:, 9216:13312]

    ld_idxt = ydma(IDXA, idxt_d[:], deps=(), sem="ixm")
    ld_maskt = ydma(MKA, maskt_d[:], deps=(), sem="ixm")
    ld_iotam2 = ydma(IOTA_A, iotam_d[:], deps=(), sem="ixm")

    dd_st = [None] * NH
    qr_tok = [None] * NH
    dda_g = None
    ddb_g = None
    ddc_g = None
    for h in range(NH):
        u1 = vop("tensor_scalar", DDA, IDXA, SNEGBC[:, h:h + 1], None, ALU.mult,
                 deps=(ld_idxt, e_sneg, dda_g))
        u2 = vop("tensor_scalar", DDB, DDA, 0.0, None, ALU.min,
                 deps=(u1, ddb_g))
        u3 = act(DDA, DDB, AF.Exp, deps=(u2,))
        dd = vop("tensor_tensor", DDC, DDA, MKA, ALU.mult,
                 deps=(u3, ld_maskt, ddc_g))
        dd_st[h] = ydma(dd_d[h], DDC, deps=(dd,), sem="ddst")
        dda_g = dd
        ddb_g = u3
        ddc_g = dd_st[h]
        qr_tok[h] = act(QR16[0:1, h * 256:(h + 1) * 256], IOTA_A, AF.Exp,
                        deps=(ld_iotam2, sneg_tok), scale=SNEGR[0:1, h:h + 1])
    dd_all = dd_st[NH - 1]
    qr_all = qr_tok[NH - 1]

    WKV = [arbf(0, 8192).rearrange("p (c j) -> p c j", c=CH),
           arbf(8192, 8192).rearrange("p (c j) -> p c j", c=CH)]
    slot_g = [(ops[4]["x1"], ops[4]["xlo"], ops[5]["x1"], ops[5]["xlo"]),
              (xsl_g[0], xsl_g[1], xlo1_g)]
    wkv_ld = [None] * 8

    def wkv_load(ob):
        g = ()
        for x_ in _tup(slot_g[ob % 2]):
            g += _tup(x_)
        return pdma(WKV[ob % 2], wkv8_d[ob].rearrange("c p j -> p c j"),
                    deps=g, sem="wkv%d" % (ob % 2))

    for ob in range(2):
        wkv_ld[ob] = wkv_load(ob)
    ev_tok = {}
    for ob in range(8):
        obk = ob if ob < 4 else ob - 4
        for tg in range(2):
            lastmm = {}
            for c in range(CH):
                for k in range(4):
                    tc = tg * 4 + k
                    bank = tg * 4 + k
                    first = (c == 0)
                    deps = [wkv_ld[ob]]
                    if first:
                        deps += [xt_tok[tg * 4 + kk] for kk in range(4)]
                    lastmm[tc] = mm(PB[bank][:, :],
                                    XTL[:, tc, c, :],
                                    WKV[ob % 2][:, c, :],
                                    first, (c == CH - 1),
                                    deps=deps, bank=bank)
            for k in range(4):
                tc = tg * 4 + k
                if ob < 4:
                    dst = KSBH[:, 4 * obk:4 * obk + 4, tc, :]
                else:
                    dst = VSB3[:, tc, 512 * obk:512 * obk + 512]
                ev_tok[(ob, tc)] = act(dst, PB[tg * 4 + k][:, :], AF.Silu,
                                       deps=(lastmm[tc],))
                bankg[tg * 4 + k] = ev_tok[(ob, tc)]
        slot_g[ob % 2] = lastmm[7]
        if ob + 2 < 8:
            wkv_ld[ob + 2] = wkv_load(ob + 2)
    kv_all = tuple(ev_tok[(ob, tc)] for ob in range(8) for tc in range(TC))
    k_done = tuple(ev_tok[(ob, tc)] for ob in range(4) for tc in range(TC))
    p1_done = kv_all[-1]
    if "ksb" in tap_d:
        ydma(tap_d["ksb"], KSB[:], deps=k_done)
    if "vsb" in tap_d:
        ydma(tap_d["vsb"], VSB[:], deps=kv_all)

    arena_g0 = _tup(slot_g[0])
    arena_g1 = _tup(slot_g[1])

    XLOT = GATE[:, 2048:10240].rearrange("p (a c t) -> p a c t", a=4, c=CH)
    WRF = GATE[:, 10240:12288].bitcast(F32)
    WRHI = GATE[:, 12288:13312]
    WRLO = GATE[:, 13312:14336]
    SEL8A = GATE[:, 14336:16384].bitcast(F32)[0:8, :]
    WRF3 = WRF.rearrange("p (c j) -> p c j", c=CH)
    WRHI3 = WRHI.rearrange("p (c j) -> p c j", c=CH)
    WRLO3 = WRLO.rearrange("p (c j) -> p c j", c=CH)

    ld_wrf = ydma(WRF3, wr16_d.rearrange("c p j -> p c j"), deps=(), sem="wr")
    wr_hi = act(WRHI, WRF, AF.Copy, deps=(ld_wrf,))
    wr_lo = vop("tensor_tensor", WRLO, WRF, WRHI, ALU.subtract, deps=(wr_hi,))
    ld_sel8 = ydma(SEL8A, sel8_d[:], deps=(), sem="sel")

    lg_mm_last = [None, None]
    prev_l2 = None
    for hf in range(2):
        tok_tr = []
        for tci in range(4):
            tc = hf * 4 + tci
            l2 = ydma(
                XLOT[:, tci],
                xlo_d[tc * 128:(tc + 1) * 128, :].rearrange("t (c p) -> t c p", c=CH),
                deps=(xlo_st[tc], prev_l2) +
                ((sq_last,) if hf == 0 and tci == 0 else ()) +
                ((lg_mm_last[0],) if hf == 1 else ()),
                transpose=True, sem="l12")
            prev_l2 = l2
            tok_tr.append(l2)
        first = True
        lastmm = None
        for c in range(CH):
            xhi = XTL[:, 4 * hf:4 * hf + 4, c, :]
            xlo = XLOT[:, :, c, :]
            whi = WRHI3[:, c, :]
            wlo = WRLO3[:, c, :]
            d0 = (wr_hi,) + xt_all + tuple(tok_tr) if first else (wr_hi,)
            lastmm = mm(PB[5 + hf][0:64, :], whi, xhi, first, False, deps=d0,
                        bank=5 + hf)
            first = False
            lastmm = mm(PB[5 + hf][0:64, :], wlo, xhi, False, False, deps=(wr_lo,))
            lastmm = mm(PB[5 + hf][0:64, :], whi, xlo, False, (c == CH - 1),
                        deps=(tok_tr[-1],))
        lg_mm_last[hf] = lastmm

    LGSB = arf32(0, 1024)[0:64, :]
    lg_tok = []
    for hf in range(2):
        lg_tok.append(act(LGSB[:, hf * 512:(hf + 1) * 512], PB[5 + hf][0:64, :],
                          AF.Copy, deps=(lg_mm_last[hf], ld_rb, wr_lo), bias=RBC[:]))
    bankg[5] = lg_tok[0]
    bankg[6] = lg_tok[1]
    if "lg" in tap_d:
        ydma(tap_d["lg"], LGSB, deps=tuple(lg_tok))

    E_t = arf32(2048, 64)
    MB_t = arbf(2176, 64)
    MS_t = arf32(2240, 64)
    MS2_t = arf32(2368, 64)
    mwt_tok = [None] * TC
    rat_tok = [None] * TC
    prev_lcp = None
    prev_ee = None
    prev_mwb = None
    for tc in range(TC):
        hf = tc // 4
        ltr = tr(PB[5][:, 0:64], LGSB[:, tc * 128:(tc + 1) * 128], IDF[0:64, 0:64],
                 deps=(lg_tok[hf], ld_idf, prev_lcp), bank=5)
        lcp = vop("tensor_copy", MS_t, PB[5][:, 0:64],
                  deps=(ltr, prev_mwb) + (arena_g0 if tc == 0 else ()))
        bankg[5] = lcp
        nmx = vop("reduce_max", TOPC[:, 0:1], MS_t, AX.X, None, None, True,
                  deps=(lcp,))
        ee = act(E_t, MS_t, AF.Exp,
                 deps=(nmx, prev_ee) + (arena_g0 if tc == 0 else ()),
                 bias=TOPC[:, 0:1])
        t8 = vop("max", TOPC[:, 2:10], E_t, deps=(ee,))
        den = vop("reduce_sum", TOPC[:, 10:11], E_t, AX.X, deps=(ee,))
        rz = vop("reciprocal", TOPC[:, 11:12], TOPC[:, 10:11], deps=(den,))
        s8 = vop("reduce_sum", TOPC[:, 12:13], TOPC[:, 2:10], AX.X, deps=(t8,))
        rat_tok[tc] = vop("tensor_tensor", RAT8[:, tc:tc + 1], TOPC[:, 12:13],
                          TOPC[:, 11:12], ALU.mult, deps=(s8, rz))
        msk = vop("tensor_scalar", MS_t, E_t, TOPC[:, 9:10], None, ALU.is_ge,
                  deps=(t8, lcp))
        mw1 = vop("tensor_tensor", MS2_t, MS_t, E_t, ALU.mult,
                  deps=(msk, ee) + (arena_g0 if tc == 0 else ()))
        mwb = vop("tensor_scalar", MB_t, MS2_t, TOPC[:, 11:12], None, ALU.mult,
                  deps=(mw1, rz, prev_mwb) + (arena_g0 if tc == 0 else ()))
        mtr = tr(PB[6][0:64, 0:256].bitcast(BF16)[:, 0:128], MB_t, IDB[:],
                 deps=(mwb, ld_idb, mwt_tok[tc - 1] if tc else None), bank=6)
        mwt_tok[tc] = vop("tensor_copy", MWT[:, tc * 128:(tc + 1) * 128],
                          PB[6][0:64, 0:256].bitcast(BF16)[:, 0:128], deps=(mtr,))
        bankg[6] = mwt_tok[tc]
        prev_lcp = lcp
        prev_ee = mw1
        prev_mwb = mwb
    rtr = tr(PB[5][0:8, 0:128], RAT8[:], IDF[:],
             deps=tuple(rat_tok) + (ld_idf, prev_lcp), bank=5)
    RRS = MIX[:, 0:256].bitcast(F32)[0:8, :]
    rcp = vop("tensor_copy", RRS[:], PB[5][0:8, 0:128], deps=(rtr,) + xn_last)
    bankg[5] = rcp
    rb_mm = None
    for tcq in range(2):
        for k in range(4):
            tc = tcq * 4 + k
            rb_mm = mm(PB[6][:, (tcq * 4 + k - tcq * 4) * 128:
                             (tcq * 4 + k - tcq * 4 + 1) * 128]
                       if False else
                       PB[6][:, k * 128:(k + 1) * 128],
                       SEL8A[:, tc * 128:(tc + 1) * 128], RRS[:],
                       k == 0, k == 3, deps=(rcp, ld_sel8), bank=6)
        ratio_tok_i = act(RATIOB[:, tcq * 512:(tcq + 1) * 512], PB[6][:, :],
                          AF.Copy, deps=(rb_mm,))
        bankg[6] = ratio_tok_i
        if tcq == 0:
            ratio_tok = [ratio_tok_i]
        else:
            ratio_tok.append(ratio_tok_i)
    if "ratio" in tap_d:
        ydma(tap_d["ratio"], RATIOB[:], deps=tuple(ratio_tok))
    if "mwt" in tap_d:
        ydma(tap_d["mwt"], MWT[:], deps=tuple(mwt_tok))

    QDBALL = arbf(11520, 4096)
    qdb_tok = [None] * NH
    qe_g = _tup(slot_g[1])
    for h in range(NH):
        qb_mm = mm(PB[4][:, 0:256], ONESRB[:], QR16[0:1, h * 256:(h + 1) * 256],
                   True, True, deps=(qr_tok[h],), bank=4)
        qdb_tok[h] = act(QDBALL[:, h * 256:(h + 1) * 256], PB[4][:, 0:256],
                         AF.Copy, deps=(qb_mm,) + qe_g)
        qe_g = ()
        bankg[4] = qdb_tok[h]

    KD = [arbf(10240, 128), arbf(10368, 128)]
    STT = [arf32(10496, 128), arf32(10752, 128)]
    SEND = [arf32(11008, 128), arf32(11264, 128)]
    chain_arena_g = arena_g1

    send_tok = [None] * NH
    stb_store = {}
    st_chain = {"stt_g": [chain_arena_g, chain_arena_g],
                "send_g": [chain_arena_g, chain_arena_g],
                "kd_g": [chain_arena_g, chain_arena_g],
                "pbU": None}

    def emit_chain_step(h, j, cst):
        if j == 0:
            cst["ms0"] = vop("memset", STT[0], 0.0, deps=_tup(cst["stt_g"][0]))
            cst["cur"], cst["oth"] = 0, 1
            cst["cur_tok"] = cst["ms0"]
        cur, oth = cst["cur"], cst["oth"]
        stb_store[(h, j)] = ydma(stb_d[h, j], STT[cur], deps=(cst["cur_tok"],),
                                 sem="stbst")
        kd0 = act(KD[0], KSBH[:, h, 2 * j, :], AF.Copy,
                  deps=(ev_tok[(h // 4, 2 * j)], kd_all) + _tup(cst["kd_g"][0]),
                  scale=KDCOL[:, 2 * h:2 * h + 1])
        kd1 = act(KD[1], KSBH[:, h, 2 * j + 1, :], AF.Copy,
                  deps=(ev_tok[(h // 4, 2 * j + 1)],) + _tup(cst["kd_g"][1]),
                  scale=KDCOL[:, 2 * h + 1:2 * h + 2])
        m1 = mm(PB[7][:, 0:128], KD[0], VSB3[:, 2 * j, 128 * h:128 * h + 128],
                True, False,
                deps=(kd0, ev_tok[(4 + h // 4, 2 * j)], cst["pbU"]), bank=7)
        m2 = mm(PB[7][:, 0:128], KD[1], VSB3[:, 2 * j + 1, 128 * h:128 * h + 128],
                False, True, deps=(kd1, ev_tok[(4 + h // 4, 2 * j + 1)]))
        cst["kd_g"] = [m1, m2]
        up = vop("scalar_tensor_tensor", STT[oth], STT[cur], BDBC[:, h:h + 1],
                 PB[7][:, 0:128], ALU.mult, ALU.add,
                 deps=(m2, cst["cur_tok"], e_bd, stb_store[(h, j)],
                       stb_store.get((h, j - 1)) if j > 0 else
                       _tup(cst["stt_g"][1])[0] if _tup(cst["stt_g"][1]) else None))
        bankg[7] = up
        cst["pbU"] = up
        cst["cur"], cst["oth"] = oth, cur
        cst["cur_tok"] = up
        if j == NBLK - 1:
            sd = vop("tensor_scalar", SEND[h % 2], STT[cst["cur"]],
                     MASKSB[:, 0:1], None, ALU.mult,
                     deps=(up, ld_masks) + _tup(cst["send_g"][h % 2]))
            send_tok[h] = ydma(ccin_d[128 * h:128 * h + 128, :], SEND[h % 2],
                               deps=(sd,), sem="sdst")
            cst["send_g"][h % 2] = send_tok[h]
            cst["stt_g"] = [sd, stb_store[(h, 3)]]

    gate_tok = {}
    wsm_g = [None, None]
    wsm_n = [0]

    def wsm_load(src_ap, deps=()):
        i = wsm_n[0] % 2
        wsm_n[0] += 1
        t = pdma(WSM[i].rearrange("p (c j) -> p c j", c=CH),
                 src_ap.rearrange("c p j -> p c j"),
                 deps=_tup(wsm_g[i]) + tuple(deps), sem="wsm%d" % i)
        return i, t

    for i_ in range(CH):
        wi, wt = wsm_load(wg16_d[i_])
        pbase = 0 if i_ % 2 == 0 else 2
        lastmm = [None, None]
        for c in range(CH):
            for th in range(2):
                first = (c == 0)
                lastmm[th] = mm(PB[pbase + th][:, :],
                                WSM[wi][:, c * 128:(c + 1) * 128],
                                XTL[:, 4 * th:4 * th + 4, c, :],
                                first, c == CH - 1,
                                deps=(wt,), bank=pbase + th)
            if c % 4 == 3:
                emit_chain_step(i_, c // 4, st_chain)
        if i_ < 2:
            gdep = (xlo1_g, prev_mwb)
        elif i_ < 10:
            gdep = (lg_mm_last[1],)
        elif i_ < 14:
            gdep = (lg_mm_last[1], wr_lo)
        else:
            gdep = (rb_mm,)
        for th in range(2):
            gate_tok[(i_, th)] = act(GATE3[:, i_, th * 512:(th + 1) * 512],
                                     PB[pbase + th][:, :], AF.Sigmoid,
                                     deps=(lastmm[th],) + gdep)
            bankg[pbase + th] = gate_tok[(i_, th)]
        wsm_g[wi] = lastmm[1]
    kd_g = st_chain["kd_g"]
    gate_all = tuple(gate_tok[(oc, th)] for oc in range(CH) for th in range(2))
    if "gate" in tap_d:
        ydma(tap_d["gate"], GATE[:], deps=gate_all)

    cc_tok = R.o("gpsimd",
                 lambda e: e.collective_compute(
                     "AllGather", ALU.bypass,
                     replica_groups=[[0, 1], [2, 3], [4, 5], [6, 7]],
                     ins=[ccin_d.opt()], outs=[ccout_d.opt()]),
                 deps=tuple(send_tok), sem="cc", n=1)
    if "ccout" in tap_d:
        ydma(tap_d["ccout"], ccout_d[:], deps=(cc_tok,))
    if "stb" in tap_d:
        ydma(tap_d["stb"], stb_d[:],
             deps=tuple(stb_store[(h, j)] for h in range(NH) for j in range(NBLK)))


    QBF = [arbf(0, 1024), arbf(1024, 1024)]
    QDQ = [arbf(2048, 1024), arbf(3072, 1024)]
    KT = [arbf(4096, 1024), arbf(5120, 1024)]
    DDH = [arbf(6144, 512), arbf(6656, 512)]
    STBH = [arf32(7680, 512), arf32(8704, 512)]
    KVH = [arf32(9728, 128), arf32(9984, 128)]
    KVR = [arf32(10240, 128), arf32(10496, 128)]
    SF = [arbf(10752, 128), arbf(10880, 128)]
    QKP = [arbf(11008, 256), arbf(11264, 256)]

    p4g0 = (dd_st[NH - 1], send_tok[NH - 1],
            stb_store[(NH - 1, 3)], kd_g[0], kd_g[1],
            prev_mwb, prev_lcp, rcp, lg_mm_last[1], ratio_tok[1])

    qbf_g = [p4g0, p4g0]
    qdq_g = [p4g0, p4g0]
    kt_g = [p4g0, p4g0]
    ddh_g = [p4g0, p4g0]
    stbh_g = [p4g0, p4g0]
    kvh_g = [p4g0, p4g0]
    kvr_g = [p4g0, p4g0]
    sf_g = [p4g0, p4g0]
    qkp_g = [p4g0, p4g0]

    head_pre = {}

    def prefetch_head(h, extra=()):
        i = h % 2
        toks = {}
        toks["w"] = wsm_load(wq16_d[h], deps=extra)
        toks["kt"] = ydma(KT[i].rearrange("d (a t) -> d a t", a=TC),
                          KSB[:, h * 1024:(h + 1) * 1024]
                          .rearrange("t (a d) -> t a d", a=TC),
                          deps=k_done + _tup(kt_g[i]) + tuple(extra),
                          transpose=True, sem="kt%d" % i)
        toks["dd"] = ydma(DDH[i], dd_d[h], deps=(dd_all,) + _tup(ddh_g[i]),
                          sem="ddl%d" % i)

        toks["stb"] = ydma(STBH[i].rearrange("p (j e) -> p j e", j=4),
                           stb_d[h].rearrange("j p e -> p j e"),
                           deps=(stb_store[(NH - 1, 3)],) + _tup(stbh_g[i]),
                           sem="stbl%d" % i)
        toks["kvr"] = ydma(KVR[i], ccout_d[128 * h:128 * h + 128, :],
                           deps=(cc_tok,) + _tup(kvr_g[i]), sem="kvr%d" % i)
        head_pre[h] = toks
        return toks

    prefetch_head(0)
    prefetch_head(1)

    mix_tok = {}
    for h in range(NH):
        i = h % 2
        pre = head_pre[h]
        wi = pre["w"][0]
        kvm = vop("tensor_scalar", KVH[i], KVR[i], MASKSB[:, 1:2], None, ALU.mult,
                  deps=(pre["kvr"], ld_masks) + _tup(kvh_g[i]))
        kvr_g[i] = kvm
        pbase = 0 if h % 2 == 0 else 2
        qlast = [None, None]
        for c in range(CH):
            for th in range(2):
                first = (c == 0)
                deps = [pre["w"][1]]
                if first:
                    deps += list(_tup(qbf_g[i]))
                qlast[th] = mm(PB[pbase + th][:, :],
                               WSM[wi][:, c * 128:(c + 1) * 128],
                               XTL[:, 4 * th:4 * th + 4, c, :],
                               first, c == CH - 1, deps=deps, bank=pbase + th)
        qsil = []
        for th in range(2):
            qsil.append(act(QBF[i][:, th * 512:(th + 1) * 512], PB[pbase + th][:, :],
                            AF.Silu, deps=(qlast[th],) + _tup(qbf_g[i])))
            bankg[pbase + th] = qsil[th]
        wsm_g[wi] = qlast[1]
        qdq = vop("tensor_tensor",
                  QDQ[i].rearrange("p (b m) -> p b m", b=NBLK),
                  QBF[i].rearrange("p (b m) -> p b m", b=NBLK),
                  QDBALL[:, h * 256:(h + 1) * 256]
                  .rearrange("p (b m) -> p b m", b=1).broadcast_to([128, NBLK, 256]),
                  ALU.mult,
                  deps=(qsil[0], qsil[1], qdb_tok[h]) + _tup(qdq_g[i]))
        if h in (0, 1):
            sfx = str(h)
            if "qbf" + sfx in tap_d:
                ydma(tap_d["qbf" + sfx], QBF[i], deps=(qsil[0], qsil[1]))
            if "kt" + sfx in tap_d:
                ydma(tap_d["kt" + sfx], KT[i], deps=(pre["kt"],))
            if h == 1 and "ddh1" in tap_d:
                ydma(tap_d["ddh1"], DDH[i], deps=(pre["dd"],))
            if h == 1 and "stbh1" in tap_d:
                ydma(tap_d["stbh1"].rearrange("p j e -> p j e"),
                     STBH[i].rearrange("p (j e) -> p j e", j=4), deps=(pre["stb"],))
        o_evs = []
        for j in range(NBLK):
            jp = j % 2
            sf = vop("scalar_tensor_tensor", SF[jp], KVH[i],
                     BDPBC[:, 4 * h + j:4 * h + j + 1],
                     STBH[i][:, 128 * j:128 * j + 128], ALU.mult, ALU.add,
                     deps=(kvm, pre["stb"], e_bdp) + _tup(sf_g[jp]))
            qks = []
            for ncn in range(2):
                qks.append(mm(PB[4 + ncn][:, 0:256],
                              KT[i][:, 256 * j + 128 * ncn:256 * j + 128 * ncn + 128],
                              QBF[i][:, 256 * j:256 * j + 256], True, True,
                              deps=(pre["kt"], qsil[0] if j < 2 else qsil[1]),
                              bank=4 + ncn))
            qkps = []
            for ncn in range(2):
                qkps.append(vop("tensor_tensor", QKP[ncn], PB[4 + ncn][:, 0:256],
                                DDH[i][:, 256 * ncn:256 * ncn + 256], ALU.mult,
                                deps=(qks[ncn], pre["dd"]) + _tup(qkp_g[ncn])))
                bankg[4 + ncn] = qkps[ncn]
            ob_ = 6 + jp
            mm(PB[ob_][:, 0:256], VSB3[:, 2 * j, 128 * h:128 * h + 128],
               QKP[0], True, False,
               deps=(qkps[0], ev_tok[(4 + h // 4, 2 * j)]), bank=ob_)
            mm(PB[ob_][:, 0:256], VSB3[:, 2 * j + 1, 128 * h:128 * h + 128],
               QKP[1], False, False, deps=(qkps[1], ev_tok[(4 + h // 4, 2 * j + 1)]))
            mo3 = mm(PB[ob_][:, 0:256], SF[jp], QDQ[i][:, 256 * j:256 * j + 256],
                     False, True, deps=(sf, qdq))
            sf_g[jp] = mo3
            qkp_g = [mo3, mo3]
            if h == 1 and j == 0 and "sf1" in tap_d:
                ydma(tap_d["sf1"], SF[jp], deps=(sf,))
            if h == 1 and j == 0 and "qkp1" in tap_d:
                ydma(tap_d["qkp1"][:, 0:256], QKP[0], deps=(qkps[0],))
                ydma(tap_d["qkp1"][:, 256:512], QKP[1], deps=(qkps[1],))
            mixg = ()
            if h < 4:
                mixg = xn_last
            elif h < 14:
                mixg = (dd_all, qdb_tok[NH - 1])
            ev = act(MIX3[:, h, 256 * j:256 * j + 256], PB[ob_][:, 0:256], AF.Copy,
                     deps=(mo3,) + mixg)
            bankg[ob_] = ev
            o_evs.append(ev)
        kt_g[i] = qks[1]
        ddh_g[i] = qkps[1]
        stbh_g[i] = sf
        kvh_g[i] = sf
        qbf_g[i] = (qks[1], qdq)
        qdq_g[i] = mo3
        if h + 2 < NH:
            prefetch_head(h + 2)
        mix_tok[h] = tuple(o_evs)
    mix_all = tuple(t for h in range(NH) for t in mix_tok[h])
    p4_done = (mix_tok[NH - 2][-1], mix_tok[NH - 1][-1],
               qdq_g[0], qdq_g[1], sf_g[0], sf_g[1])
    if "mix" in tap_d:
        ydma(tap_d["mix"], MIX[:], deps=mix_all)

    SQ = [arbf(0, 1024), arbf(1024, 1024)]
    EBB = arbf(2048, 2048)[0:64, :]
    GA = arbf(4096, 1024)
    GB = arf32(6144, 1024)
    VROW = arf32(8192, 1024)[0:1, :]
    RROW = arf32(10240, 1024)[0:1, :]
    ZT = arbf(12288, 1024)

    ld_eb = pdma(EBB, eb_d[:], deps=p4_done, sem="ebl")

    sq_all = [None] * NH
    g_tok = [None] * CH
    z_tok = [None] * CH
    var_mm = None
    b_last = None
    ga_g = p4_done
    gb_g = p4_done
    zt_g = p4_done
    for i in range(CH):
        pbase = 3 if i % 2 == 0 else 5
        bmmt = [None, None]
        for th in range(2):
            bmmt[th] = mm(PB[pbase + th][:, :], EBB[:, i * 128:(i + 1) * 128],
                          MWT[:, th * 512:(th + 1) * 512], True, True,
                          deps=(ld_eb, mwt_tok[-1]), bank=pbase + th)
            b_last = bmmt[th]
        ga = vop("tensor_tensor", GA.rearrange("p (a t) -> p a t", a=TC),
                 XTL[:, :, i, :],
                 RATIOB.rearrange("p (a t) -> p a t", a=TC), ALU.mult,
                 deps=(ratio_tok[0], ratio_tok[1]) + _tup(ga_g))
        gbs = []
        for th in range(2):
            gbs.append(vop("tensor_tensor", GB[:, th * 512:(th + 1) * 512],
                           GA[:, th * 512:(th + 1) * 512], PB[pbase + th][:, :],
                           ALU.add, deps=(ga, bmmt[th]) + _tup(gb_g)))
            bankg[pbase + th] = gbs[th]
        g_tok[i] = act(VSBL[:, :, i, :], GB, AF.Silu,
                       deps=tuple(gbs) + (mix_all[-1], p4_done[-1]))
        ga_g = gbs[1]
        gb_g = g_tok[i]
        sq_all[i] = act(SQ[i % 2], MIX3[:, i, :], AF.Square,
                        deps=mix_tok[i] + p4_done + ((var_mm,) if i >= 2 else ()))
        for th in range(2):
            var_mm = mm(PB[th][0:1, :], INVD[:],
                        SQ[i % 2][:, th * 512:(th + 1) * 512],
                        i == 0, i == CH - 1,
                        deps=(sq_all[i], ld_invd), bank=th)
        zt = vop("tensor_tensor", ZT, MIX3[:, i, :], GATE3[:, i, :], ALU.mult,
                 deps=(sq_all[i], gate_tok[(i, 0)], gate_tok[(i, 1)]) +
                 mix_tok[i] + _tup(zt_g))
        z_tok[i] = vop("tensor_scalar", MIX3[:, i, :], ZT, NWC[:, i:i + 1],
                       None, ALU.mult, deps=(zt, ld_nw, sq_all[i]))
        zt_g = z_tok[i]

    vcp = []
    for th in range(2):
        vcp.append(vop("tensor_copy", VROW[:, th * 512:(th + 1) * 512],
                       PB[th][0:1, :], deps=(var_mm,) + p4_done))
    bankg[0] = vcp[0]
    bankg[1] = vcp[1]
    veps = vop("tensor_scalar", RROW, VROW, RMS_EPS, None, ALU.add,
               deps=tuple(vcp) + p4_done)
    vln = act(VROW, RROW, AF.Ln, deps=(veps,))
    vlnm = vop("tensor_scalar", RROW, VROW, -0.5, None, ALU.mult, deps=(vln,))
    vrs = act(VROW, RROW, AF.Exp, deps=(vlnm,))
    rc_mm = None
    for tc in range(TC):
        rc_mm = mm(PB[2][:, tc:tc + 1], VROW[0:1, tc * 128:(tc + 1) * 128], ONE1[:],
                   tc == 0, tc == TC - 1, deps=(vrs, ld_one1), bank=2)
    rstd_tok = vop("tensor_copy", RSTDC[:], PB[2][:, 0:8], deps=(rc_mm,))
    bankg[2] = rstd_tok
    if "rstd" in tap_d:
        ydma(tap_d["rstd"], RSTDC[:], deps=(rstd_tok,))
    g_all = tuple(g_tok)

    gt_tok = [None] * TC
    for tc in range(TC):
        gt_tok[tc] = ydma(KSB[:, tc * 2048:(tc + 1) * 2048]
                          .rearrange("p (c f) -> p c f", c=CH),
                          VSB[:, tc * 2048:(tc + 1) * 2048]
                          .rearrange("p (c t) -> p c t", c=CH),
                          deps=g_all + (head_pre[NH - 1]["kt"], kd_g[1]),
                          transpose=True, sem="gtt")
    if "gt" in tap_d:
        ydma(tap_d["gt"], KSB[:], deps=tuple(gt_tok))

    WO = [arbf(0, 2048), arbf(2048, 2048)]
    WOF = arf32(4096, 2048)
    XH = [arf32(8192, 1024), arf32(8192, 1024)]
    YV = [arf32(10240, 1024), arf32(12288, 1024)]
    PO = [arf32(14336, 1024), arf32(14336, 1024)]

    wo_g = [(var_mm, sq_all[15], ld_eb, b_last),
            (var_mm, sq_all[15], ld_eb, b_last)]
    wof_g = (var_mm, sq_all[15]) + _tup(gb_g) + _tup(ga_g)
    xh_g = [(rstd_tok,), (rstd_tok,)]
    yv_g = [(vrs, rstd_tok, g_tok[15]), (vrs, rstd_tok, g_tok[15], zt_g)]
    po_g = [(vrs, rstd_tok, zt_g), (vrs, rstd_tok, zt_g)]
    out_st = []
    for fh in range(2):
        for tq in range(2):
            lastmm = {}
            wts = {}
            for op in range(8):
                wi = op % 2
                if wi == 0:
                    wts[op] = pdma(WO[0], wo8_d[fh, op], deps=_tup(wo_g[0]),
                                   sem="wol0")
                else:
                    wf = ydma(WOF, wo8_d[fh, op],
                              deps=_tup(wo_g[1]) + _tup(wof_g), sem="xh0")
                    wts[op] = vop("tensor_copy", WO[1], WOF, deps=(wf,))
                    wof_g = wts[op]
                for oi in range(2):
                    oc = 2 * op + oi
                    for ti in range(4):
                        tc = 4 * tq + ti
                        for q in range(2):
                            bank = ti * 2 + q
                            first = (oc == 0)
                            lastmm[(ti, q)] = mm(
                                PB[bank][:, :],
                                MIX3[:, oc, tc * 128:(tc + 1) * 128],
                                WO[wi][:, oi * 1024 + q * 512:oi * 1024 + (q + 1) * 512],
                                first, oc == CH - 1,
                                deps=(wts[op], z_tok[oc]), bank=bank)
                wo_g[wi] = lastmm[(3, 1)]
            for ti in range(4):
                tc = 4 * tq + ti
                ip = ti % 2
                xh = ydma(XH[ip], xs_d[tc * 128:(tc + 1) * 128,
                                       fh * 1024:(fh + 1) * 1024],
                          deps=_tup(xh_g[0]) + _tup(xh_g[1]), sem="xh1")
                yv = []
                for q in range(2):
                    yv.append(act(YV[ip][:, q * 512:(q + 1) * 512],
                                  PB[ti * 2 + q][:, :], AF.Copy,
                                  deps=(lastmm[(ti, q)], rstd_tok) + _tup(yv_g[ip]),
                                  scale=RSTDC[:, tc:tc + 1]))
                    bankg[ti * 2 + q] = yv[q]
                if fh == 0 and tq == 0 and ti == 0 and "yv00" in tap_d:
                    ydma(tap_d["yv00"], YV[ip], deps=tuple(yv))
                pm = vop("tensor_tensor", PO[ip], YV[ip],
                         KSB[:, tc * 2048 + fh * 1024:tc * 2048 + fh * 1024 + 1024],
                         ALU.mult,
                         deps=tuple(yv) + (gt_tok[TC - 1],) + _tup(po_g[ip]))
                po = vop("tensor_tensor", YV[ip], PO[ip], XH[ip], ALU.add,
                         deps=(pm, xh))
                po_g[ip] = po
                st = ydma(out_d[tc * 128:(tc + 1) * 128, fh * 1024:(fh + 1) * 1024],
                          YV[ip], deps=(po,), sem="ost%d" % ip)
                out_st.append(st)
                yv_g[ip] = (po, st)
                xh_g[ip] = (po,)

    final_counts = {k: R.counts[k] for k in R.counts if R.counts[k] > 0}

    for engname, ops in R.stream.items():
        for fn, deps, sem, n in ops:
            for dsem, dval in deps:
                assert dval <= R.counts[dsem], (engname, dsem, dval, R.counts[dsem])

    def make_body(engname):
        def body(eng):
            last = {}
            for fn, deps, sem, n in R.stream[engname]:
                for dsem, dval in deps:
                    if last.get(dsem, 0) < dval:
                        eng.wait_ge(sems[dsem], dval)
                        last[dsem] = dval
                inst = fn(eng)
                inst.then_inc(sems[sem], n)
            if engname == "sync":
                for kk in ("ost0", "ost1", "y"):
                    if final_counts.get(kk):
                        eng.wait_ge(sems[kk], final_counts[kk])
        return body

    with nc.Block() as block:
        block.sync(make_body("sync"))
        block.gpsimd(make_body("gpsimd"))
        block.scalar(make_body("scalar"))
        block.vector(make_body("vector"))
        block.tensor(make_body("tensor"))

    for g in reversed(psg):
        g.__exit__(None, None, None)
    for g in reversed(sbg):
        g.__exit__(None, None, None)
    return nc


def _shared_arrays(wqkv, wgate, wout, wrout, rbias, ebias, normw, slope):
    f32 = np.float32
    rows_k = (384 * np.arange(NH)[:, None] + 128 + np.arange(HD)[None, :]).reshape(-1)
    rows_v = (384 * np.arange(NH)[:, None] + 256 + np.arange(HD)[None, :]).reshape(-1)
    rows_q = (384 * np.arange(NH)[:, None] + np.arange(HD)[None, :]).reshape(-1)
    wsel = wqkv[np.concatenate([rows_k, rows_v])]
    wkv8 = np.ascontiguousarray(
        wsel.reshape(8, 512, 16, 128).transpose(0, 2, 3, 1))
    wq16 = np.ascontiguousarray(
        wqkv[rows_q].reshape(16, 128, 16, 128).transpose(0, 2, 3, 1))
    wg16 = np.ascontiguousarray(
        wgate.reshape(16, 128, 16, 128).transpose(0, 2, 3, 1))
    woT = np.ascontiguousarray(wout.T)
    wo8 = np.ascontiguousarray(
        woT.reshape(8, 2, 128, 2, 1024).transpose(3, 0, 2, 1, 4)
        .reshape(2, 8, 128, 2048))
    wr16 = np.ascontiguousarray(wrout.T.reshape(16, 128, 64))
    rb = np.ascontiguousarray(rbias.reshape(64, 1)).astype(f32)
    nw = np.ascontiguousarray(normw.reshape(16, 128).T).astype(f32)
    slopes = np.ascontiguousarray(slope.reshape(1, 16)).astype(f32)

    m = np.arange(1, 257, dtype=f32)
    identb = np.eye(128, dtype=NPBF)
    identf = np.eye(128, dtype=f32)
    iotam = m.reshape(1, 256).copy()
    jneg = np.tile(np.arange(4, dtype=f32) * (-256.0), 16).reshape(1, 64)
    iotanm = np.stack([np.arange(1, 129, dtype=f32) - 256.0,
                       np.arange(129, 257, dtype=f32) - 256.0], axis=1)
    sel8 = np.zeros((8, 1024), dtype=f32)
    for k in range(8):
        sel8[k, k * 128:(k + 1) * 128] = 1.0
    invd = np.full((128, 1), 1.0 / H, dtype=NPBF)
    p = np.arange(128, dtype=f32)
    idxt = np.zeros((128, 512), dtype=f32)
    for jc in range(2):
        idxt[:, jc * 256:(jc + 1) * 256] = (m[None, :] - 1.0) - (128 * jc + p[:, None])
    maskt = (idxt >= 0).astype(f32)
    onesr = np.ones((1, 128), dtype=f32)
    one1 = np.ones((1, 1), dtype=f32)
    return dict(wkv8=wkv8.astype(f32), wq16=wq16.astype(f32), wg16=wg16.astype(f32),
                wo8=wo8.astype(f32), wr16=wr16.astype(f32), rb=rb,
                eb=np.ascontiguousarray(ebias).astype(f32), nw=nw, slopes=slopes,
                identb=identb, identf=identf, iotam=iotam, jneg=jneg.astype(f32),
                iotanm=iotanm.astype(f32), sel8=sel8, invd=invd,
                idxt=idxt, maskt=maskt, onesr=onesr, one1=one1,
                onesrb=np.ones((1, 128), dtype=NPBF))


def make_in_maps(inputs):
    x = np.asarray(inputs["L0_attn_hidden_states"], dtype=np.float32)
    shared = _shared_arrays(
        np.asarray(inputs["L0_attn_w_qkv"], np.float32),
        np.asarray(inputs["L0_attn_w_gate"], np.float32),
        np.asarray(inputs["L0_attn_w_out"], np.float32),
        np.asarray(inputs["L0_ffn_router_weight"], np.float32),
        np.asarray(inputs["L0_ffn_router_bias"], np.float32),
        np.asarray(inputs["L0_ffn_expert_bias"], np.float32),
        np.asarray(inputs["L0_attn_norm_w"], np.float32),
        np.asarray(inputs["L0_attn_slope_rate"], np.float32),
    )
    in_maps = []
    for core in range(8):
        b, half = core >> 1, core & 1
        mdict = dict(shared)
        mdict["xs"] = np.ascontiguousarray(x[b, half * T:(half + 1) * T, :])
        send = 1.0 if half == 0 else 0.0
        recv = 0.0 if half == 0 else 1.0
        mdict["masks"] = np.tile(np.array([[send, recv]], np.float32), (128, 1))
        in_maps.append(mdict)
    return in_maps


def kernel(**inputs):
    in_maps = make_in_maps(inputs)
    nc = build()
    res = run_bass_kernel_spmd(nc, in_maps, core_ids=list(range(8)))
    out = np.empty((4, 2048, H), dtype=np.float32)
    for core in range(8):
        b, half = core >> 1, core & 1
        out[b, half * T:(half + 1) * T, :] = res.results[core]["out"]
    return out
